# version 10
# baseline (speedup 1.0000x reference)
import numpy as np
import concourse.bass as bass
import concourse.mybir as mybir
from concourse.bass_utils import run_bass_kernel_spmd
from concourse.aot_env import allow_ici_all_to_all

F32 = mybir.dt.float32
F32R = mybir.dt.float32r
BF16 = mybir.dt.bfloat16
FP8 = mybir.dt.float8e4
U8 = mybir.dt.uint8
AF = mybir.ActivationFunctionType
DRM = mybir.MatmulPerfMode.DoubleRow
ALU = mybir.AluOpType
AX = mybir.AxisListType

H = 2048
S = 2048
NB = 4
TOK = 1024
NORM = float(1.0 / np.sqrt(128.0))
R448 = 1.0 / 448.0
ARENA = 180224
ESZ = {F32: 4, F32R: 4, BF16: 2, FP8: 1, U8: 1}

O_XNBF = 0
O_QKV = 65536
O_XB0, O_XB1 = 65536, 73728
O_XSQ0, O_XSQ1 = 81920, 90112
O_TB = 98304
O_XNF0, O_XNF1 = 106496, 114688
O_RBC, O_MRBC = 122880, 131072
O_R2BC, O_MR2BC = 139264, 143360
O_X2B0, O_X2B1 = 147456, 151552
O_XN2F0, O_XN2F1 = 155648, 159744
O_WM0, O_WM1 = 122880, 131072
O_WQ0, O_WQ1 = 163840, 167936
O_WSLF0, O_WSLF1 = 172032, 174080
O_WSL80, O_WSL81 = 176128, 176640
O_VTM = 0
O_LB0, O_LB1 = 0, 8192
O_LT0, O_LT1 = 16384, 24576
O_X8 = 32768
O_XNH0, O_XNH1 = 49152, 53248
O_PT = (49152, 51200, 53248, 55296)
O_RDBC = 57344
O_CTXF = 61440
O_CTXB = 172032
O_W8FC0, O_W8FC1 = 65536, 67584
O_ZB0, O_ZB1 = 69632, 73728
O_SFM = 0
O_CTXL = 32768
O_H8 = 65536
O_WPR0, O_WPR1 = 131072, 135168
O_ZL0, O_ZL1 = 139264, 143360
O_TFM = 32768
O_W8P0, O_W8P1 = 131072, 139264
O_GB0, O_GB1 = 147456, 151552
O_XT0, O_XT1 = 155648, 163840
O_OUT = 172032


class Sem:
    __slots__ = ("h", "n")

    def __init__(self, h):
        self.h = h
        self.n = 0


def _build():
    nc = bass.Bass("TRN2", target_bir_lowering=False, debug=False)

    x_tma = nc.dram_tensor("x_tma", [S, H], F32, kind="ExternalInput")
    x_tm = nc.dram_tensor("x_tm", [TOK, H], F32, kind="ExternalInput")
    wqkvT = nc.dram_tensor("wqkvT", [H, 3072], F32, kind="ExternalInput")
    wpr = nc.dram_tensor("wpr", [H, H], F32, kind="ExternalInput")
    bproj_cols = nc.dram_tensor("bproj_cols", [128, 16], F32, kind="ExternalInput")
    wfc_sl = nc.dram_tensor("wfc_sl", [1024, 2048], F32, kind="ExternalInput")
    wpj_sl = nc.dram_tensor("wpj_sl", [256, 8192], F32, kind="ExternalInput")
    ident = nc.dram_tensor("ident", [128, 128], F32, kind="ExternalInput")
    out = nc.dram_tensor("out", [TOK, H], F32, kind="ExternalOutput")

    z_d = nc.dram_tensor("z_d", [8192, TOK], F32)
    a2a_in = nc.dram_tensor("a2a_in", [2048, TOK], BF16)
    a2a_out = nc.dram_tensor("a2a_out", [2048, TOK], BF16)
    agfc_in = nc.dram_tensor("agfc_in", [1024, 2048], FP8)
    agfc_out = nc.dram_tensor("agfc_out", [8192, 2048], FP8)
    agpj_in = nc.dram_tensor("agpj_in", [256, 8192], FP8)
    agpj_out = nc.dram_tensor("agpj_out", [2048, 8192], FP8)
    ar1_in = nc.dram_tensor("ar1_in", [8, 1], F32)
    ar1_out = nc.dram_tensor("ar1_out", [8, 1], F32)
    ar2_in = nc.dram_tensor("ar2_in", [8, 1], F32)
    ar2_out = nc.dram_tensor("ar2_out", [8, 1], F32)
    flat_d = nc.dram_tensor("flat_d", [512], F32)
    drow_d = nc.dram_tensor("drow_d", [8192], F32)

    prog = {k: [] for k in ("sync", "gpsimd", "scalar", "vector", "tensor")}

    def em(eng, fn):
        prog[eng].append(fn)

    def dview(t, off, n):
        return bass.AP(t, off, [[n, 1], [1, n]])

    with (
        nc.sbuf_tensor("arena", [128, ARENA], U8) as arena_t,
        nc.sbuf_tensor("ident_bf", [128, 128], BF16) as ident_bf,
        nc.sbuf_tensor("ones_f", [128, 1], F32) as ones_f,
        nc.sbuf_tensor("ones_b", [128, 1], BF16) as ones_b,
        nc.sbuf_tensor("bpc", [128, 16], F32) as bpc,
        nc.sbuf_tensor("scal_bc", [128, 8], F32) as scal_bc,
        nc.sbuf_tensor("rows", [1, 6144], F32) as rows_t,
        nc.sbuf_tensor("srow", [1, 64], F32) as srow,
        nc.sbuf_tensor("red8", [128, 1], F32) as red8,
        nc.sbuf_tensor("runx", [128, 1], F32) as runx,
        nc.sbuf_tensor("runf", [128, 1], F32) as runf,
        nc.sbuf_tensor("runp", [128, 1], F32) as runp,
        nc.sbuf_tensor("runz", [128, 1], F32) as runz,
        nc.sbuf_tensor("redt", [128, 1], F32) as redt,
        nc.sbuf_tensor("flat3", [4, 128], F32) as flat3,
        nc.sbuf_tensor("ident_f", [128, 128], F32) as ident_f,
        nc.sbuf_tensor("ln_t", [128, 64], F32) as ln_t,
        nc.psum_tensor("PS", [128, 4096], F32) as PS,
        nc.Block() as blk,
    ):
        arena = arena_t.ap()

        def AV(off, dtype, n):
            assert off % 512 == 0 and off + n * ESZ[dtype] <= ARENA, (off, n)
            return arena[:, off: off + n * ESZ[dtype]].bitcast(dtype)

        SN = [Sem(nc.alloc_semaphore(name=f"S{i}")) for i in range(60)]
        (s_misc, s_lx0, s_wm0, s_bc, s_x20, s_xnh0, s_wq0, s_sbc, s_x8ld0, s_wslld0,
         s_rdb, s_ctxst, s_fc0, s_zst0, s_zld0, s_pj0, s_wpr0, s_ctxld, s_xtm0,
         s_out, c_xsq, c_stat, c_misc, c_xn, c_qk, c_qev, c_exp, c_av,
         c_ctxv, c_zrun, c_h8, c_wpev, c_sc2, c_g, c_tr, cc_ar1, cc_fc,
         cc_pj, cc_a2a, cc_ar2, c_ss, c_sss,
         s_lx1, s_wm1, s_x21, s_xnh1, s_wq1, s_x8ld1, s_wslld1, s_fc1,
         s_zst1, s_zld1, s_pj1, s_wpr1, s_xtm1, s_wslst0, s_wslst1,
         _sp0, _sp1, _sp2) = SN
        s_lx = (s_lx0, s_lx1)
        s_wm = (s_wm0, s_wm1)
        s_x2 = (s_x20, s_x21)
        s_xnh = (s_xnh0, s_xnh1)
        s_wq = (s_wq0, s_wq1)
        s_x8ld = (s_x8ld0, s_x8ld1)
        s_wslld = (s_wslld0, s_wslld1)
        s_wslst = (s_wslst0, s_wslst1)
        s_fc = (s_fc0, s_fc1)
        s_zst = (s_zst0, s_zst1)
        s_zld = (s_zld0, s_zld1)
        s_pj = (s_pj0, s_pj1)
        s_wpr = (s_wpr0, s_wpr1)
        s_xtm = (s_xtm0, s_xtm1)

        def vss(fn):
            c_ss.n += 1
            em("vector", lambda v, fn=fn: fn(v).then_inc(c_ss.h, 1))
            em("vector", lambda v, n=c_ss.n: v.wait_ge(c_ss.h, n))

        def sss(fn):
            c_sss.n += 1
            em("scalar", lambda s, fn=fn: fn(s).then_inc(c_sss.h, 1))
            em("scalar", lambda s, n=c_sss.n: s.wait_ge(c_sss.h, n))

        def vsig(fn, sem, k=1):
            vss(fn)
            sem.n += k
            em("vector", lambda v, sem=sem, k=k: v.engine_nop().then_inc(sem.h, k))

        def ssig(fn, sem, k=1):
            sss(fn)
            sem.n += k
            em("scalar", lambda s, sem=sem, k=k: s.activation(srow[0:1, 48:49], srow[0:1, 48:49], AF.Identity).then_inc(sem.h, k))

        rowA = rows_t.ap()[0:1, 0:2048]
        rowB = rows_t.ap()[0:1, 2048:4096]
        rowC = rows_t.ap()[0:1, 4096:6144]

        stt = ln_t.ap()[:, 0:24]
        ag = ln_t.ap()[:, 24:26]
        vt = ln_t.ap()[:, 26:27]
        st0 = ln_t.ap()[:, 27:28]
        rr = ln_t.ap()[:, 28:29]
        nmr = ln_t.ap()[:, 29:30]
        qkv = AV(O_QKV, BF16, 49152)
        xn_bf = AV(O_XNBF, BF16, 32768)

        def ph_consts():
            em("gpsimd", lambda g: g.dma_start(out=ident_bf[:, :], in_=ident[:, :]).then_inc(s_bc.h, 16))
            s_bc.n += 16
            em("gpsimd", lambda g: g.memset(ones_f[:, :], 1.0))
            em("gpsimd", lambda g: g.memset(red8[:, :], 0.0))
            em("gpsimd", lambda g: g.memset(srow[0:1, :], 0.0))
            em("gpsimd", lambda g: g.memset(ones_b[:, :], 1.0).then_inc(c_misc.h, 1))
            c_misc.n += 1
            em("sync", lambda e: e.dma_start(out=bpc[:, :], in_=bproj_cols[:, :]).then_inc(s_misc.h, 16))
            s_misc.n += 16
            em("sync", lambda e: e.dma_start(out=ident_f[:, :], in_=ident[:, :]).then_inc(s_misc.h, 16))
            s_misc.n += 16
            tb = s_bc.n
            tm = s_misc.n
            tc = c_misc.n
            em("tensor", lambda x, t=tb: x.wait_ge(s_bc.h, t))
            em("tensor", lambda x, t=tm: x.wait_ge(s_misc.h, t))
            em("tensor", lambda x, t=tc: x.wait_ge(c_misc.h, t))

        def ph_wmax():
            for t_i, (src, pieces, run) in enumerate(
                    ((wfc_sl, [(c, 0) for c in range(8)], runf),
                     (wpj_sl, [(c, q) for c in range(2) for q in range(4)], runp))):
                for pi, (c, q) in enumerate(pieces):
                    buf = AV(O_WM0 if pi % 2 == 0 else O_WM1, F32, 2048)
                    if not (t_i == 0 and pi < 2):
                        v = c_xsq.n - 1
                        em("sync", lambda e, v=v: e.wait_ge(c_xsq.h, v))
                    sw = s_wm[pi % 2]
                    def ld(e, src=src, c=c, q=q, buf=buf, sw=sw):
                        e.dma_start(out=buf, in_=src[c * 128:(c + 1) * 128, q * 2048:(q + 1) * 2048]).then_inc(sw.h, 16)
                    em("sync", ld)
                    sw.n += 16
                    tw = sw.n
                    em("vector", lambda v, t=tw, sw=sw: v.wait_ge(sw.h, t))
                    first = (pi == 0)
                    if first:
                        vsig(lambda v, buf=buf, run=run: v.reduce_max(run[:, 0:1], buf, axis=AX.X, apply_absolute_value=True), c_xsq)
                    else:
                        vss(lambda v, buf=buf: v.reduce_max(redt[:, 0:1], buf, axis=AX.X, apply_absolute_value=True))
                        vsig(lambda v, run=run: v.tensor_max(run[:, 0:1], run[:, 0:1], redt[:, 0:1]), c_xsq)

        def ph_lnA():
            nm_mark = {}
            ax_mark = {}
            def absmax(c):
                xt = AV(O_TB if c % 2 == 0 else O_XNF0, F32, 2048)
                em("vector", lambda v, t=nm_mark[c]: v.wait_ge(c_misc.h, t))
                if c == 0:
                    vsig(lambda v, xt=xt: v.reduce_max(runx[:, 0:1], xt, axis=AX.X, apply_absolute_value=True), c_xn)
                else:
                    vss(lambda v, xt=xt: v.reduce_max(redt[:, 0:1], xt, axis=AX.X, apply_absolute_value=True))
                    vsig(lambda v: v.tensor_max(runx[:, 0:1], runx[:, 0:1], redt[:, 0:1]), c_xn)
                ax_mark[c] = c_xn.n
            for c in range(16):
                p = 32 * (c % 2)
                sttp = ln_t.ap()[:, p + 0:p + 24]
                agp = ln_t.ap()[:, p + 24:p + 26]
                vtp = ln_t.ap()[:, p + 26:p + 27]
                st0p = ln_t.ap()[:, p + 27:p + 28]
                rrp = ln_t.ap()[:, p + 28:p + 29]
                nmrp = ln_t.ap()[:, p + 29:p + 30]
                xb = AV(O_XB0 if c % 2 == 0 else O_XB1, F32, 2048)
                xt = AV(O_TB if c % 2 == 0 else O_XNF0, F32, 2048)
                sl = s_lx[c % 2]
                if c >= 2:
                    v = ax_mark[c - 2]
                    em("sync", lambda e, v=v: e.wait_ge(c_xn.h, v))
                def ld(e, c=c, xb=xb, sl=sl):
                    e.dma_start(out=xb, in_=x_tma[c * 128:(c + 1) * 128, :]).then_inc(sl.h, 16)
                em("sync", ld)
                sl.n += 16
                tl = sl.n
                em("vector", lambda v, t=tl, sl=sl: v.wait_ge(sl.h, t))
                if c >= 2:
                    v = nm_mark[c - 2]
                    em("vector", lambda w, v=v: w.wait_ge(c_misc.h, v))
                for q in range(3):
                    em("vector", lambda v, q=q, xb=xb, sttp=sttp: v.bn_stats(sttp[:, q * 6:(q + 1) * 6], xb[:, q * 512:(q + 1) * 512]))
                vss(lambda v, xb=xb, sttp=sttp: v.bn_stats(sttp[:, 18:24], xb[:, 1536:2048]))
                vss(lambda v, agp=agp, sttp=sttp: v.bn_aggr(agp[:, :], sttp[:, :]))
                vss(lambda v, vtp=vtp, agp=agp: v.tensor_scalar_add(vtp[:, 0:1], agp[:, 1:2], 1e-5))
                c_xsq.n += 1
                em("vector", lambda v: v.engine_nop().then_inc(c_xsq.h, 1))
                ts0 = c_xsq.n
                em("scalar", lambda s, t=ts0: s.wait_ge(c_xsq.h, t))
                c_sc2.n += 1
                em("scalar", lambda s, st0p=st0p, vtp=vtp: s.activation(st0p[:, 0:1], vtp[:, 0:1], AF.Sqrt).then_inc(c_sc2.h, 1))
                ts1 = c_sc2.n
                em("vector", lambda v, t=ts1: v.wait_ge(c_sc2.h, t))
                vss(lambda v, rrp=rrp, st0p=st0p: v.reciprocal(rrp[:, 0:1], st0p[:, 0:1]))
                vss(lambda v, nmrp=nmrp, rrp=rrp: v.tensor_mul(nmrp[:, 0:1], rrp[:, 0:1], rrp[:, 0:1]))
                vss(lambda v, nmrp=nmrp, vtp=vtp: v.tensor_mul(nmrp[:, 0:1], nmrp[:, 0:1], vtp[:, 0:1]))
                vss(lambda v, nmrp=nmrp: v.tensor_scalar(nmrp[:, 0:1], nmrp[:, 0:1], -0.5, 1.5, ALU.mult, ALU.add))
                vss(lambda v, rrp=rrp, nmrp=nmrp: v.tensor_mul(rrp[:, 0:1], rrp[:, 0:1], nmrp[:, 0:1]))
                vss(lambda v, nmrp=nmrp, agp=agp, rrp=rrp: v.tensor_mul(nmrp[:, 0:1], agp[:, 0:1], rrp[:, 0:1]))
                c_g.n += 1
                em("vector", lambda v, nmrp=nmrp: v.tensor_scalar_mul(nmrp[:, 0:1], nmrp[:, 0:1], -1.0).then_inc(c_g.h, 1))
                tn = c_g.n
                em("scalar", lambda s, t=tn: s.wait_ge(c_g.h, t))
                if c >= 2:
                    v = c_qk.n - 1
                    em("scalar", lambda s, v=v: s.wait_ge(c_qk.h, v))
                if c == 0:
                    em("scalar", lambda s, v=c_misc.n: s.wait_ge(c_misc.h, v))
                c_misc.n += 1
                em("scalar", lambda s, xb=xb, xt=xt, rrp=rrp, nmrp=nmrp: s.activation(
                    xt, xb, AF.Identity, scale=rrp[:, 0:1], bias=nmrp[:, 0:1]).then_inc(c_misc.h, 1))
                tm = c_misc.n
                nm_mark[c] = tm
                if c >= 1:
                    absmax(c - 1)
                em("tensor", lambda x, t=tm: x.wait_ge(c_misc.h, t))
                if c >= 2:
                    v = c_qev.n - 1
                    em("tensor", lambda x, v=v: x.wait_ge(c_qev.h, v))
                pst = PS[:, (c % 2) * 2048:(c % 2) * 2048 + 2048]
                for f in range(16):
                    last = (f == 15)
                    def tr(x, c=c, f=f, xt=xt, pst=pst, last=last):
                        ins = x.transpose(pst[:, f * 128:(f + 1) * 128],
                                          xt[:, f * 128:(f + 1) * 128], ident_f[:, :])
                        if last:
                            ins.then_inc(c_qk.h, 1)
                    em("tensor", tr)
                c_qk.n += 1
                tq = c_qk.n
                em("scalar", lambda s, t=tq: s.wait_ge(c_qk.h, t))
                c_qev.n += 1
                def ev(s, c=c, pst=pst):
                    s.activation(xn_bf.rearrange("p (f n) -> p f n", n=2048)[:, :, c * 128:(c + 1) * 128],
                                 pst.rearrange("p (f n) -> p f n", n=128), AF.Identity).then_inc(c_qev.h, 1)
                em("scalar", ev)
            absmax(15)

        def ph_ar1():
            t0 = c_xn.n
            em("gpsimd", lambda g, t=t0: g.wait_ge(c_xn.h, t))
            em("gpsimd", lambda g: g.dma_start(out=flat_d[0:128], in_=runx[:, 0:1]).then_inc(s_rdb.h, 16))
            em("gpsimd", lambda g: g.dma_start(out=flat_d[128:256], in_=runf[:, 0:1]).then_inc(s_rdb.h, 16))
            em("gpsimd", lambda g: g.dma_start(out=flat_d[256:384], in_=runp[:, 0:1]).then_inc(s_rdb.h, 16))
            s_rdb.n += 48
            t1 = s_rdb.n
            em("gpsimd", lambda g, t=t1: g.wait_ge(s_rdb.h, t))
            em("gpsimd", lambda g: g.dma_start(out=flat3[0:3, :], in_=bass.AP(flat_d, 0, [[128, 3], [1, 128]])).then_inc(s_rdb.h, 16))
            s_rdb.n += 16
            t2 = s_rdb.n
            em("vector", lambda v, t=t2: v.wait_ge(s_rdb.h, t))
            vsig(lambda v: v.reduce_max(red8[0:3, 0:1], flat3[0:3, :], axis=AX.X), c_misc)
            t3 = c_misc.n
            em("gpsimd", lambda g, t=t3: g.wait_ge(c_misc.h, t))
            em("gpsimd", lambda g: g.dma_start(out=ar1_in[:, :], in_=red8[0:8, 0:1]).then_inc(s_rdb.h, 16))
            s_rdb.n += 16
            t4 = s_rdb.n
            em("gpsimd", lambda g, t=t4: g.wait_ge(s_rdb.h, t))
            cc_ar1.n += 1
            em("gpsimd", lambda g: g.collective_compute(
                "AllReduce", ALU.max, replica_groups=[list(range(8))],
                ins=[ar1_in.ap()], outs=[ar1_out.ap()]).then_inc(cc_ar1.h, 1))

        def ph_ar1_fin():
            em("gpsimd", lambda g: g.wait_ge(cc_ar1.h, 1))
            em("gpsimd", lambda g: g.dma_start(out=srow[0:1, 0:8], in_=dview(ar1_out, 0, 8)).then_inc(s_sbc.h, 16))
            s_sbc.n += 16
            t5 = s_sbc.n
            em("vector", lambda v, t=t5: v.wait_ge(s_sbc.h, t))
            vss(lambda v: v.tensor_scalar(srow[0:1, 0:3], srow[0:1, 0:3], R448, 1e-6, ALU.mult, ALU.max))
            vss(lambda v: v.reciprocal(srow[0:1, 4:7], srow[0:1, 0:3]))
            vss(lambda v: v.tensor_scalar_mul(srow[0:1, 4:7], srow[0:1, 4:7], 0.5))
            vss(lambda v: v.tensor_mul(srow[0:1, 7:8], srow[0:1, 0:1], srow[0:1, 1:2]))
            vss(lambda v: v.tensor_scalar_mul(srow[0:1, 7:8], srow[0:1, 7:8], 4.0))
            vss(lambda v: v.tensor_copy(srow[0:1, 16:19], srow[0:1, 4:7]))
            vsig(lambda v: v.tensor_copy(srow[0:1, 19:20], srow[0:1, 7:8]), c_misc)
            t6 = c_misc.n
            em("gpsimd", lambda g, t=t6: g.wait_ge(c_misc.h, t))
            em("gpsimd", lambda g: g.dma_start(out=drow_d[4096:4104], in_=srow[0:1, 16:24]).then_inc(s_sbc.h, 16))
            s_sbc.n += 16
            t7 = s_sbc.n
            em("gpsimd", lambda g, t=t7: g.wait_ge(s_sbc.h, t))
            em("gpsimd", lambda g: g.dma_start(out=scal_bc[:, 0:8], in_=dview(drow_d, 4096, 8).partition_broadcast(128)).then_inc(s_sbc.h, 16))
            s_sbc.n += 16

        def ph_lnB():
            nmb_mark = {}
            x8 = AV(O_X8, FP8, 16384)
            tsb = s_sbc.n
            em("scalar", lambda s, t=tsb: s.wait_ge(s_sbc.h, t))
            for c in range(8):
                xb = AV(O_LB0 if c % 2 == 0 else O_LB1, F32, 2048)
                xt = AV(O_LT0 if c % 2 == 0 else O_LT1, F32, 2048)
                sl = s_x2[c % 2]
                if c < 2:
                    v = c_qev.n
                    em("sync", lambda e, v=v: e.wait_ge(c_qev.h, v))
                else:
                    v = c_misc.n - 1
                    em("sync", lambda e, v=v: e.wait_ge(c_misc.h, v))
                def ld(e, c=c, xb=xb, sl=sl):
                    e.dma_start(out=xb, in_=x_tm[c * 128:(c + 1) * 128, :]).then_inc(sl.h, 16)
                em("sync", ld)
                sl.n += 16
                tl = sl.n
                em("vector", lambda v, t=tl, sl=sl: v.wait_ge(sl.h, t))
                p = 32 * (c % 2)
                sttp = ln_t.ap()[:, p + 0:p + 24]
                agp = ln_t.ap()[:, p + 24:p + 26]
                vtp = ln_t.ap()[:, p + 26:p + 27]
                st0p = ln_t.ap()[:, p + 27:p + 28]
                rrp = ln_t.ap()[:, p + 28:p + 29]
                nmrp = ln_t.ap()[:, p + 29:p + 30]
                if c >= 2:
                    v = nmb_mark[c - 2]
                    em("vector", lambda w, v=v: w.wait_ge(c_misc.h, v))
                for q in range(3):
                    em("vector", lambda v, q=q, xb=xb, sttp=sttp: v.bn_stats(sttp[:, q * 6:(q + 1) * 6], xb[:, q * 512:(q + 1) * 512]))
                vss(lambda v, xb=xb, sttp=sttp: v.bn_stats(sttp[:, 18:24], xb[:, 1536:2048]))
                vss(lambda v, agp=agp, sttp=sttp: v.bn_aggr(agp[:, :], sttp[:, :]))
                vss(lambda v, vtp=vtp, agp=agp: v.tensor_scalar_add(vtp[:, 0:1], agp[:, 1:2], 1e-5))
                c_xsq.n += 1
                em("vector", lambda v: v.engine_nop().then_inc(c_xsq.h, 1))
                ts0 = c_xsq.n
                em("scalar", lambda s, t=ts0: s.wait_ge(c_xsq.h, t))
                c_sc2.n += 1
                em("scalar", lambda s, st0p=st0p, vtp=vtp: s.activation(st0p[:, 0:1], vtp[:, 0:1], AF.Sqrt).then_inc(c_sc2.h, 1))
                ts1 = c_sc2.n
                em("vector", lambda v, t=ts1: v.wait_ge(c_sc2.h, t))
                vss(lambda v, rrp=rrp, st0p=st0p: v.reciprocal(rrp[:, 0:1], st0p[:, 0:1]))
                vss(lambda v, nmrp=nmrp, rrp=rrp: v.tensor_mul(nmrp[:, 0:1], rrp[:, 0:1], rrp[:, 0:1]))
                vss(lambda v, nmrp=nmrp, vtp=vtp: v.tensor_mul(nmrp[:, 0:1], nmrp[:, 0:1], vtp[:, 0:1]))
                vss(lambda v, nmrp=nmrp: v.tensor_scalar(nmrp[:, 0:1], nmrp[:, 0:1], -0.5, 1.5, ALU.mult, ALU.add))
                vss(lambda v, rrp=rrp, nmrp=nmrp: v.tensor_mul(rrp[:, 0:1], rrp[:, 0:1], nmrp[:, 0:1]))
                vss(lambda v, nmrp=nmrp, agp=agp, rrp=rrp: v.tensor_mul(nmrp[:, 0:1], agp[:, 0:1], rrp[:, 0:1]))
                c_g.n += 1
                em("vector", lambda v, nmrp=nmrp: v.tensor_scalar_mul(nmrp[:, 0:1], nmrp[:, 0:1], -1.0).then_inc(c_g.h, 1))
                tn = c_g.n
                em("scalar", lambda s, t=tn: s.wait_ge(c_g.h, t))
                if c >= 2:
                    v = c_qk.n - 1
                    em("scalar", lambda s, v=v: s.wait_ge(c_qk.h, v))
                if c == 0:
                    em("scalar", lambda s, v=c_misc.n: s.wait_ge(c_misc.h, v))
                c_misc.n += 1
                em("scalar", lambda s, xb=xb, xt=xt, rrp=rrp, nmrp=nmrp: s.activation(
                    xt, xb, AF.Identity, scale=rrp[:, 0:1], bias=nmrp[:, 0:1]).then_inc(c_misc.h, 1))
                tm = c_misc.n
                nmb_mark[c] = tm
                em("tensor", lambda x, t=tm: x.wait_ge(c_misc.h, t))
                if c >= 1:
                    v = c_qev.n - 1 + 1
                    em("tensor", lambda x, v=v: x.wait_ge(c_qev.h, v))
                for f in range(16):
                    off = 1024 + (f % 8) * 128 if f < 8 else 3072 + (f % 8) * 128
                    last = (f == 15)
                    def tr(x, c=c, f=f, off=off, xt=xt, last=last):
                        ins = x.transpose(PS[:, off:off + 128],
                                          xt[:, f * 128:(f + 1) * 128], ident_f[:, :])
                        if last:
                            ins.then_inc(c_qk.h, 1)
                    em("tensor", tr)
                c_qk.n += 1
                tq = c_qk.n
                em("scalar", lambda s, t=tq: s.wait_ge(c_qk.h, t))
                x8v = x8.rearrange("p (f n) -> p f n", n=1024)
                def qz(s, c=c, x8v=x8v):
                    s.activation(x8v[:, 0:8, c * 128:(c + 1) * 128],
                                 PS.rearrange("p (f n) -> p f n", n=128)[:, 8:16, :],
                                 AF.Copy, scale=scal_bc[:, 0:1])
                em("scalar", qz)
                c_qev.n += 1
                def qz2(s, c=c, x8v=x8v):
                    s.activation(x8v[:, 8:16, c * 128:(c + 1) * 128],
                                 PS.rearrange("p (f n) -> p f n", n=128)[:, 24:32, :],
                                 AF.Copy, scale=scal_bc[:, 0:1]).then_inc(c_qev.h, 1)
                em("scalar", qz2)

        wq_mark = {}
        qk_mark = {}
        def q_ld(m):
            wb = AV(O_WQ0 if m % 2 == 0 else O_WQ1, BF16, 2048)
            sw = s_wq[m % 2]
            def ldw(g, m=m, wb=wb, sw=sw):
                src = bass.AP(wqkvT, m * 128, [[3072, 128], [3072 * 128, 16], [1, 128]])
                g.dma_start(out=wb, in_=src).then_inc(sw.h, 16)
            em("gpsimd", ldw)
            sw.n += 16
            wq_mark[m] = sw.n
        def ph_q_pre():
            q_ld(0)
            q_ld(1)
        def ph_q():
            for m in range(24):
                wb = AV(O_WQ0 if m % 2 == 0 else O_WQ1, BF16, 2048)
                sw = s_wq[m % 2]
                tw = wq_mark[m]
                em("tensor", lambda x, t=tw, sw=sw: x.wait_ge(sw.h, t))
                if m == 0:
                    tx = c_xn.n
                    em("tensor", lambda x, t=tx: x.wait_ge(c_xn.h, t))
                    tev = c_qev.n
                    em("tensor", lambda x, t=tev: x.wait_ge(c_qev.h, t))
                if m >= 2:
                    v = c_qev.n - 1
                    em("tensor", lambda x, v=v: x.wait_ge(c_qev.h, v))
                for c in range(16):
                    for sh in range(2):
                        for j in range(2):
                            k = 2 * (m % 2) + sh
                            last = (c == 15 and sh == 1 and j == 1)
                            def mm(x, m=m, c=c, sh=sh, j=j, k=k, wb=wb, last=last):
                                ins = x.matmul(
                                    PS[:, 1024 * k + 512 * j: 1024 * k + 512 * (j + 1)],
                                    wb[:, c * 128:(c + 1) * 128],
                                    xn_bf[:, 2048 * c + 1024 * sh + 512 * j: 2048 * c + 1024 * sh + 512 * (j + 1)],
                                    start=(c == 0), stop=(c == 15), skip_group_check=True)
                                if last:
                                    ins.then_inc(c_qk.h, 1)
                            em("tensor", mm)
                c_qk.n += 1
                qk_mark[m] = c_qk.n
                if m + 2 < 24:
                    em("gpsimd", lambda g, v=qk_mark[m]: g.wait_ge(c_qk.h, v))
                    q_ld(m + 2)
                if m == 6:
                    ph_ar1_fin()
                tq = c_qk.n
                em("vector", lambda v, t=tq: v.wait_ge(c_qk.h, t))

                for sh in range(2):
                    k = 2 * (m % 2) + sh
                    last = (sh == 1)
                    def ev(v, m=m, sh=sh, k=k, last=last):
                        ins = v.tensor_copy(qkv[:, 2048 * m + 1024 * sh: 2048 * m + 1024 * (sh + 1)],
                                            PS[:, 1024 * k:1024 * (k + 1)])
                        if last:
                            ins.then_inc(c_ss.h, 1)
                    em("vector", ev)
                c_ss.n += 1
                em("vector", lambda v, n=c_ss.n: v.wait_ge(c_ss.h, n))
                c_qev.n += 1
                em("vector", lambda v: v.engine_nop().then_inc(c_qev.h, 1))

        def ph_wq8():
            tsb = s_sbc.n
            em("scalar", lambda s, t=tsb: s.wait_ge(s_sbc.h, t))
            pieces = []
            for t_i, (src, dst, nch, ncols, scol) in enumerate(
                    ((wfc_sl, agfc_in, 8, 2048, 1), (wpj_sl, agpj_in, 2, 8192, 2))):
                for c in range(nch):
                    for q in range(ncols // 512):
                        pieces.append((src, dst, c, q, scol))
            ld_mark = {}
            qz_mark = {}
            def wq_load(n):
                src, dst, c, q, scol = pieces[n]
                par = n % 2
                fbuf = AV(O_WSLF0 if par == 0 else O_WSLF1, F32, 512)
                slp = s_wslld[par]
                if n >= 2:
                    em("sync", lambda e, v=qz_mark[n - 2]: e.wait_ge(c_sss.h, v))
                def ld(e, src=src, c=c, q=q, fbuf=fbuf, slp=slp):
                    e.dma_start(out=fbuf, in_=src[c * 128:(c + 1) * 128, q * 512:(q + 1) * 512]).then_inc(slp.h, 16)
                em("sync", ld)
                slp.n += 16
                ld_mark[n] = slp.n
            wq_load(0)
            wq_load(1)
            for n, (src, dst, c, q, scol) in enumerate(pieces):
                par = n % 2
                fbuf = AV(O_WSLF0 if par == 0 else O_WSLF1, F32, 512)
                qbuf = AV(O_WSL80 if par == 0 else O_WSL81, FP8, 512)
                if n >= 2:
                    ssp0 = s_wslst[par]
                    v = ssp0.n
                    em("scalar", lambda s, v=v, ssp0=ssp0: s.wait_ge(ssp0.h, v))
                slp = s_wslld[par]
                em("scalar", lambda s, t=ld_mark[n], slp=slp: s.wait_ge(slp.h, t))
                sss(lambda s, fbuf=fbuf, qbuf=qbuf, scol=scol: s.activation(qbuf, fbuf, AF.Copy, scale=scal_bc[:, scol:scol + 1]))
                qz_mark[n] = c_sss.n
                ssp = s_wslst[par]
                def st(s, dst=dst, c=c, q=q, qbuf=qbuf, ssp=ssp):
                    s.dma_start(out=dst[c * 128:(c + 1) * 128, q * 512:(q + 1) * 512], in_=qbuf).then_inc(ssp.h, 16)
                em("scalar", st)
                ssp.n += 16
                if n + 2 < len(pieces):
                    wq_load(n + 2)
            for ssp in s_wslst:
                tall = ssp.n
                em("gpsimd", lambda g, t=tall, ssp=ssp: g.wait_ge(ssp.h, t))
            cc_fc.n += 1
            em("gpsimd", lambda g: g.collective_compute(
                "AllGather", ALU.bypass, replica_groups=[list(range(8))],
                ins=[agfc_in.ap()], outs=[agfc_out.ap()]).then_inc(cc_fc.h, 1))
            cc_pj.n += 1
            em("gpsimd", lambda g: g.collective_compute(
                "AllGather", ALU.bypass, replica_groups=[list(range(8))],
                ins=[agpj_in.ap()], outs=[agpj_out.ap()]).then_inc(cc_pj.h, 1))

        def ph_minit():
            tqe = c_qev.n - 8
            em("tensor", lambda x, t=tqe: x.wait_ge(c_qev.h, t))
            tmc = c_misc.n
            em("vector", lambda v, t=tmc: v.wait_ge(c_misc.h, t))
            v_tm = AV(O_VTM, BF16, 16384)
            for l in range(8):
                m = 3 * l + 2
                pst = PS[:, (l % 2) * 2048:(l % 2) * 2048 + 1024].bitcast(BF16)
                if l >= 2:
                    v = c_av.n - 1
                    em("tensor", lambda x, v=v: x.wait_ge(c_av.h, v))
                for kc in range(16):
                    last = (kc == 15)
                    def tr(x, m=m, kc=kc, pst=pst, last=last):
                        ins = x.transpose(pst[:, kc * 128:(kc + 1) * 128],
                                          qkv[:, 2048 * m + kc * 128: 2048 * m + (kc + 1) * 128],
                                          ident_bf[:, :])
                        if last:
                            ins.then_inc(c_qk.h, 1)
                    em("tensor", tr)
                c_qk.n += 1
                tq = c_qk.n
                em("vector", lambda v, t=tq: v.wait_ge(c_qk.h, t))
                c_av.n += 1
                em("vector", lambda v, l=l, pst=pst, v_tm=v_tm: v.tensor_copy(
                    v_tm[:, l * 2048:(l + 1) * 2048], pst).then_inc(c_av.h, 1))

        def ph_att():
            v_tm = AV(O_VTM, BF16, 16384)
            tvt = c_av.n
            em("tensor", lambda x, t=tvt: x.wait_ge(c_av.h, t))
            em("tensor", lambda x, t=c_qev.n: x.wait_ge(c_qev.h, t))
            d_row = rowC[:, 0:1024]
            for l in range(8):
                for qh in range(2):
                    base_exp = c_exp.n
                    base_av = c_av.n
                    base_qk = c_qk.n
                    if (l, qh) != (0, 0):
                        v = c_ctxv.n
                        em("tensor", lambda x, v=v: x.wait_ge(c_ctxv.h, v))

                    def av_pair(r2):
                        for kd in (2 * r2, 2 * r2 + 1):
                            pt = AV(O_PT[kd % 4], BF16, 1024)
                            for j in range(2):
                                def avmm(x, l=l, kd=kd, j=j, pt=pt):
                                    x.matmul(PS[:, 2048 + 512 * j: 2048 + 512 * (j + 1)],
                                             v_tm[:, 2048 * l + kd * 128: 2048 * l + (kd + 1) * 128],
                                             pt[:, 512 * j: 512 * (j + 1)],
                                             start=(kd == 0), stop=(kd == 15), skip_group_check=True)
                                em("tensor", avmm)
                            for j in range(2):
                                last = (kd % 2 == 1 and j == 1)
                                def dmm(x, kd=kd, j=j, pt=pt, last=last):
                                    ins = x.matmul(PS[0:1, 3072 + 512 * j: 3072 + 512 * (j + 1)],
                                                   ones_b[:, 0:1],
                                                   pt[:, 512 * j: 512 * (j + 1)],
                                                   start=(kd == 0), stop=(kd == 15), skip_group_check=True)
                                    if last:
                                        ins.then_inc(c_av.h, 1)
                                em("tensor", dmm)
                        c_av.n += 1

                    for r in range(8):
                        if r >= 1:
                            v = base_exp + 2 * r
                            em("tensor", lambda x, v=v: x.wait_ge(c_exp.h, v))
                        for kc in (2 * r, 2 * r + 1):
                            for j in range(2):
                                last = (kc % 2 == 1 and j == 1)
                                def qkmm(x, l=l, qh=qh, kc=kc, j=j, last=last):
                                    ins = x.matmul(
                                        PS[:, (kc % 2) * 1024 + 512 * j:(kc % 2) * 1024 + 512 * (j + 1)],
                                        qkv[:, 2048 * (3 * l + 1) + kc * 128: 2048 * (3 * l + 1) + (kc + 1) * 128],
                                        qkv[:, 2048 * (3 * l) + 1024 * qh + 512 * j: 2048 * (3 * l) + 1024 * qh + 512 * (j + 1)],
                                        start=True, stop=True, skip_group_check=True)
                                    if last:
                                        ins.then_inc(c_qk.h, 1)
                                em("tensor", qkmm)
                        c_qk.n += 1
                        if r >= 1:
                            av_pair(r - 1)
                    v = base_exp + 16
                    em("tensor", lambda x, v=v: x.wait_ge(c_exp.h, v))
                    av_pair(7)
                    for r in range(8):
                        v = base_qk + r + 1
                        em("scalar", lambda s, v=v: s.wait_ge(c_qk.h, v))
                        if r >= 2:
                            v2 = base_av + r - 1
                            em("scalar", lambda s, v=v2: s.wait_ge(c_av.h, v))
                        for kc in (2 * r, 2 * r + 1):
                            pt = AV(O_PT[kc % 4], BF16, 1024)
                            c_exp.n += 1
                            em("scalar", lambda s, kc=kc, pt=pt: s.activation(
                                pt, PS[:, (kc % 2) * 1024:(kc % 2) * 1024 + 1024], AF.Exp,
                                scale=NORM).then_inc(c_exp.h, 1))
                    tav = c_av.n
                    em("vector", lambda v, t=tav: v.wait_ge(c_av.h, t))
                    ctxf = AV(O_CTXF, F32, 1024)
                    if (l, qh) != (0, 0):
                        tcs = s_ctxst.n
                        em("vector", lambda v, t=tcs: v.wait_ge(s_ctxst.h, t))
                    em("vector", lambda v, ctxf=ctxf: v.tensor_copy(ctxf, PS[:, 2048:3072]))
                    vss(lambda v: v.tensor_copy(d_row, PS[0:1, 3072:4096]))
                    c_ctxv.n += 1
                    em("vector", lambda v: v.engine_nop().then_inc(c_ctxv.h, 1))
                    if (l, qh) == (0, 0):
                        em("vector", lambda v, t=c_misc.n: v.wait_ge(c_misc.h, t))
                    vsig(lambda v: v.reciprocal(d_row, d_row), c_misc)
                    tcv = c_misc.n
                    em("gpsimd", lambda g, t=tcv: g.wait_ge(c_misc.h, t))
                    em("gpsimd", lambda g: g.dma_start(out=drow_d[6656:7680], in_=d_row).then_inc(s_rdb.h, 16))
                    s_rdb.n += 16
                    tr1 = s_rdb.n
                    em("gpsimd", lambda g, t=tr1: g.wait_ge(s_rdb.h, t))
                    rdbc = AV(O_RDBC, F32, 1024)
                    em("gpsimd", lambda g, rdbc=rdbc: g.dma_start(out=rdbc, in_=dview(drow_d, 6656, 1024).partition_broadcast(128)).then_inc(s_rdb.h, 16))
                    s_rdb.n += 16
                    tr2 = s_rdb.n
                    em("vector", lambda v, t=tr2: v.wait_ge(s_rdb.h, t))
                    ctxb = AV(O_CTXB, BF16, 1024)
                    vsig(lambda v, ctxf=ctxf, rdbc=rdbc, ctxb=ctxb: v.tensor_mul(ctxb, ctxf, rdbc), c_misc)
                    tcb = c_misc.n
                    em("sync", lambda e, t=tcb: e.wait_ge(c_misc.h, t))
                    jdst = 4 * qh + (l % 4)
                    roff = 256 * jdst + 128 * (l // 4)
                    def stc(e, roff=roff, ctxb=ctxb):
                        e.dma_start(out=a2a_in[roff:roff + 128, :], in_=ctxb).then_inc(s_ctxst.h, 16)
                    em("sync", stc)
                    s_ctxst.n += 16
            tall = s_ctxst.n
            em("gpsimd", lambda g, t=tall: g.wait_ge(s_ctxst.h, t))
            cc_a2a.n += 1
            def a2a(g):
                with allow_ici_all_to_all():
                    g.collective_compute("AllToAll", ALU.bypass,
                                         replica_groups=[list(range(8))],
                                         ins=[a2a_in.ap()], outs=[a2a_out.ap()]).then_inc(cc_a2a.h, 1)
            em("gpsimd", a2a)

        def ph_f1():
            x8 = AV(O_X8, FP8, 16384)
            em("sync", lambda e: e.wait_ge(cc_fc.h, 1))
            tq = c_av.n
            em("sync", lambda e, t=tq: e.wait_ge(c_av.h, t))
            em("tensor", lambda x, t=tq: x.wait_ge(c_av.h, t))
            tcx = c_ctxv.n
            em("tensor", lambda x, t=tcx: x.wait_ge(c_ctxv.h, t))
            tx8 = c_qev.n
            em("tensor", lambda x, t=tx8: x.wait_ge(c_qev.h, t))
            for hh in range(64):
                wb = AV(O_W8FC0 if hh % 2 == 0 else O_W8FC1, FP8, 2048)
                if hh >= 2:
                    v = c_qk.n - 1
                    em("sync", lambda e, v=v: e.wait_ge(c_qk.h, v))
                sf = s_fc[hh % 2]
                def ldw(e, hh=hh, wb=wb, sf=sf):
                    e.dma_start(out=wb, in_=agfc_out[hh * 128:(hh + 1) * 128, :]).then_inc(sf.h, 16)
                em("sync", ldw)
                sf.n += 16
                tw = sf.n
                em("tensor", lambda x, t=tw, sf=sf: x.wait_ge(sf.h, t))
                if hh >= 2:
                    v = c_zrun.n - 1
                    em("tensor", lambda x, v=v: x.wait_ge(c_zrun.h, v))
                wb3 = wb.rearrange("p (c i m) -> p c i m", i=2, m=128)
                x83 = x8.rearrange("p (c i n) -> p c i n", i=2, n=1024)
                for c in range(8):
                    for j in range(2):
                        last = (c == 7 and j == 1)
                        def mm(x, hh=hh, c=c, j=j, wb3=wb3, x83=x83, last=last):
                            ins = x.matmul(
                                PS[:, (hh % 2) * 1024 + 512 * j:(hh % 2) * 1024 + 512 * (j + 1)],
                                wb3[:, c],
                                x83[:, c, :, 512 * j: 512 * (j + 1)],
                                start=(c == 0), stop=(c == 7), perf_mode=DRM,
                                skip_group_check=True)
                            if last:
                                ins.then_inc(c_qk.h, 1)
                        em("tensor", mm)
                c_qk.n += 1
                tqk = c_qk.n
                em("scalar", lambda s, t=tqk: s.wait_ge(c_qk.h, t))
                zb = AV(O_ZB0 if hh % 2 == 0 else O_ZB1, F32, 1024)
                if hh >= 2:
                    szp = s_zst[hh % 2]
                    v = szp.n
                    em("scalar", lambda s, v=v, szp=szp: s.wait_ge(szp.h, v))
                    v2 = c_zrun.n - 1
                    em("scalar", lambda s, v=v2: s.wait_ge(c_zrun.h, v))
                if hh == 0:
                    em("scalar", lambda s, v=c_misc.n: s.wait_ge(c_misc.h, v))
                c_misc.n += 1
                em("scalar", lambda s, hh=hh, zb=zb: s.activation(
                    zb, PS[:, (hh % 2) * 1024:(hh % 2) * 1024 + 1024], AF.Relu,
                    scale=scal_bc[:, 3:4]).then_inc(c_misc.h, 1))
                tmw = c_misc.n
                em("scalar", lambda s, t=tmw: s.wait_ge(c_misc.h, t))
                szp = s_zst[hh % 2]
                def st(s, hh=hh, zb=zb, szp=szp):
                    s.dma_start(out=z_d[hh * 128:(hh + 1) * 128, :], in_=zb).then_inc(szp.h, 16)
                em("scalar", st)
                szp.n += 16
                tmv = szp.n
                em("vector", lambda v, t=tmv, szp=szp: v.wait_ge(szp.h, t))
                if hh == 0:
                    vsig(lambda v, zb=zb: v.reduce_max(runz[:, 0:1], zb, axis=AX.X), c_zrun)
                else:
                    vss(lambda v, zb=zb: v.reduce_max(redt[:, 0:1], zb, axis=AX.X))
                    vsig(lambda v: v.tensor_max(runz[:, 0:1], runz[:, 0:1], redt[:, 0:1]), c_zrun)

        def ph_ar2():
            t0 = c_zrun.n
            em("gpsimd", lambda g, t=t0: g.wait_ge(c_zrun.h, t))
            em("gpsimd", lambda g: g.dma_start(out=flat_d[0:128], in_=runz[:, 0:1]).then_inc(s_rdb.h, 16))
            s_rdb.n += 16
            t1 = s_rdb.n
            em("gpsimd", lambda g, t=t1: g.wait_ge(s_rdb.h, t))
            em("gpsimd", lambda g: g.dma_start(out=flat3[0:1, :], in_=dview(flat_d, 0, 128)).then_inc(s_rdb.h, 16))
            s_rdb.n += 16
            t2 = s_rdb.n
            em("vector", lambda v, t=t2: v.wait_ge(s_rdb.h, t))
            vss(lambda v: v.reduce_max(red8[0:1, 0:1], flat3[0:1, :], axis=AX.X))
            em("vector", lambda v, t=c_misc.n: v.wait_ge(c_misc.h, t))
            vsig(lambda v: v.tensor_mul(red8[0:1, 0:1], red8[0:1, 0:1], red8[0:1, 0:1]), c_misc)
            t3 = c_misc.n
            em("gpsimd", lambda g, t=t3: g.wait_ge(c_misc.h, t))
            em("gpsimd", lambda g: g.dma_start(out=ar2_in[:, :], in_=red8[0:8, 0:1]).then_inc(s_rdb.h, 16))
            s_rdb.n += 16
            t4 = s_rdb.n
            em("gpsimd", lambda g, t=t4: g.wait_ge(s_rdb.h, t))
            cc_ar2.n += 1
            em("gpsimd", lambda g: g.collective_compute(
                "AllReduce", ALU.max, replica_groups=[list(range(8))],
                ins=[ar2_in.ap()], outs=[ar2_out.ap()]).then_inc(cc_ar2.h, 1))
            em("gpsimd", lambda g: g.wait_ge(cc_ar2.h, 1))
            em("gpsimd", lambda g: g.dma_start(out=srow[0:1, 32:40], in_=dview(ar2_out, 0, 8)).then_inc(s_sbc.h, 16))
            s_sbc.n += 16
            t5 = s_sbc.n
            em("vector", lambda v, t=t5: v.wait_ge(s_sbc.h, t))
            vss(lambda v: v.tensor_scalar(srow[0:1, 32:33], srow[0:1, 32:33], R448, 1e-6, ALU.mult, ALU.max))
            vss(lambda v: v.reciprocal(srow[0:1, 33:34], srow[0:1, 32:33]))
            vss(lambda v: v.tensor_scalar_mul(srow[0:1, 33:34], srow[0:1, 33:34], 0.5))
            vss(lambda v: v.tensor_mul(srow[0:1, 34:35], srow[0:1, 32:33], srow[0:1, 2:3]))
            c_misc.n += 1
            em("vector", lambda v: v.tensor_scalar_mul(srow[0:1, 34:35], srow[0:1, 34:35], 4.0).then_inc(c_misc.h, 1))
            c_ss.n += 1
            em("vector", lambda v: v.engine_nop().then_inc(c_ss.h, 1))
            em("vector", lambda v, n=c_ss.n: v.wait_ge(c_ss.h, n))
            t6 = c_misc.n
            em("scalar", lambda s, t=t6: s.wait_ge(c_misc.h, t))
            c_sc2.n += 1
            em("scalar", lambda s: s.activation(srow[0:1, 35:36], srow[0:1, 33:34], AF.Sqrt).then_inc(c_sc2.h, 1))
            t7 = c_sc2.n
            em("vector", lambda v, t=t7: v.wait_ge(c_sc2.h, t))
            for _ in range(2):
                vss(lambda v: v.reciprocal(srow[0:1, 36:37], srow[0:1, 35:36]))
                vss(lambda v: v.tensor_mul(srow[0:1, 36:37], srow[0:1, 36:37], srow[0:1, 33:34]))
                vss(lambda v: v.tensor_add(srow[0:1, 35:36], srow[0:1, 35:36], srow[0:1, 36:37]))
                vss(lambda v: v.tensor_scalar_mul(srow[0:1, 35:36], srow[0:1, 35:36], 0.5))
            vss(lambda v: v.tensor_copy(srow[0:1, 20:21], srow[0:1, 35:36]))
            vsig(lambda v: v.tensor_copy(srow[0:1, 21:22], srow[0:1, 34:35]), c_misc)
            t8 = c_misc.n
            em("gpsimd", lambda g, t=t8: g.wait_ge(c_misc.h, t))
            em("gpsimd", lambda g: g.dma_start(out=drow_d[4352:4354], in_=srow[0:1, 20:22]).then_inc(s_sbc.h, 16))
            s_sbc.n += 16
            t9 = s_sbc.n
            em("gpsimd", lambda g, t=t9: g.wait_ge(s_sbc.h, t))
            em("gpsimd", lambda g: g.dma_start(out=scal_bc[:, 4:6], in_=dview(drow_d, 4352, 2).partition_broadcast(128)).then_inc(s_sbc.h, 16))
            s_sbc.n += 16

        wpr_tgt = {}
        def ph_wp_loads(ocs, wait_region=False):
            if wait_region:
                em("gpsimd", lambda g, t=c_xn.n: g.wait_ge(c_xn.h, t))
            for oc in ocs:
                wb = AV(O_WPR0 if oc % 2 == 0 else O_WPR1, BF16, 2048)
                if oc >= 2:
                    v = c_wpev.n + (oc - 2) + 1
                    em("gpsimd", lambda g, v=v: g.wait_ge(c_wpev.h, v))
                sw = s_wpr[oc % 2]
                def ldw(g, oc=oc, wb=wb, sw=sw):
                    g.dma_start(out=wb, in_=wpr[oc * 128:(oc + 1) * 128, :]).then_inc(sw.h, 16)
                em("gpsimd", ldw)
                sw.n += 16
                wpr_tgt[oc] = sw.n

        def ph_wp():
            ctxl = AV(O_CTXL, BF16, 16384)
            sfm = AV(O_SFM, BF16, 16384)
            em("sync", lambda e: e.wait_ge(cc_a2a.h, 1))
            tf1 = c_qk.n
            em("sync", lambda e, t=tf1: e.wait_ge(c_qk.h, t))
            for ih in range(16):
                roff = 256 * (ih // 2) + 128 * (ih % 2)
                def ldc(e, ih=ih, roff=roff, ctxl=ctxl):
                    e.dma_start(out=ctxl[:, ih * 1024:(ih + 1) * 1024], in_=a2a_out[roff:roff + 128, :]).then_inc(s_ctxld.h, 16)
                em("sync", ldc)
                s_ctxld.n += 16
            tzr = c_zrun.n
            em("tensor", lambda x, t=tzr: x.wait_ge(c_zrun.h, t))
            for oc in range(16):
                wb = AV(O_WPR0 if oc % 2 == 0 else O_WPR1, BF16, 2048)
                sw = s_wpr[oc % 2]
                tw = wpr_tgt[oc]
                tc = s_ctxld.n
                em("tensor", lambda x, t=tw, sw=sw: x.wait_ge(sw.h, t))
                if oc == 0:
                    em("tensor", lambda x, t=tc: x.wait_ge(s_ctxld.h, t))
                if oc >= 2:
                    v = c_wpev.n - 1
                    em("tensor", lambda x, v=v: x.wait_ge(c_wpev.h, v))
                for fc in range(16):
                    for j in range(2):
                        last = (fc == 15 and j == 1)
                        def mm(x, oc=oc, fc=fc, j=j, wb=wb, ctxl=ctxl, last=last):
                            ins = x.matmul(
                                PS[:, (oc % 2) * 1024 + 512 * j:(oc % 2) * 1024 + 512 * (j + 1)],
                                wb[:, fc * 128:(fc + 1) * 128],
                                ctxl[:, 1024 * fc + 512 * j: 1024 * fc + 512 * (j + 1)],
                                start=(fc == 0), stop=(fc == 15), skip_group_check=True)
                            if last:
                                ins.then_inc(c_qk.h, 1)
                        em("tensor", mm)
                c_qk.n += 1
                tqk = c_qk.n
                em("scalar", lambda s, t=tqk: s.wait_ge(c_qk.h, t))
                c_wpev.n += 1
                em("scalar", lambda s, oc=oc, sfm=sfm: s.activation(
                    sfm[:, oc * 1024:(oc + 1) * 1024],
                    PS[:, (oc % 2) * 1024:(oc % 2) * 1024 + 1024],
                    AF.Identity, bias=bpc[:, oc:oc + 1]).then_inc(c_wpev.h, 1))

        O_ZL4A, O_ZL4B = 147456, 163840
        def ph_h8():
            h8 = AV(O_H8, FP8, 65536)
            HSLOT = (O_ZL4A, O_ZL4B, O_ZL4A + 8192, 139264)
            hsem = (s_zld0, s_zld1, s_lx0, s_zst0)
            heng = ("sync", "scalar", "sync", "scalar")
            h8_mark = {}
            hl_mark = {}
            def h8_ld(g):
                zb = AV(HSLOT[g % 4], F32, 2048)
                sl = hsem[g % 4]
                def ld(e, g=g, zb=zb, sl=sl):
                    e.dma_start(out=zb, in_=bass.AP(z_d, g * 2 * 128 * 1024,
                                                    [[1024, 128], [128 * 1024, 2], [1, 1024]])).then_inc(sl.h, 16)
                em(heng[g % 4], ld)
                sl.n += 16
                hl_mark[g] = sl.n
            for szp in s_zst:
                tz = szp.n
                em("sync", lambda e, t=tz, szp=szp: e.wait_ge(szp.h, t))
                em("scalar", lambda s, t=tz, szp=szp: s.wait_ge(szp.h, t))
            for g in range(4):
                h8_ld(g)
            tq = s_sbc.n
            em("scalar", lambda s, t=tq: s.wait_ge(s_sbc.h, t))
            for g in range(32):
                zb = AV(HSLOT[g % 4], F32, 2048)
                tl = hl_mark[g]
                sl = hsem[g % 4]
                em("scalar", lambda s, t=tl, sl=sl: s.wait_ge(sl.h, t))
                qz = lambda s, g=g, zb=zb, h8=h8: s.activation(
                    h8[:, g * 2048:(g + 1) * 2048], zb, AF.Square,
                    scale=scal_bc[:, 4:5])
                if g == 31:
                    ssig(qz, c_h8, k=2)
                elif g + 4 < 32 and heng[g % 4] == "scalar":
                    ssig(qz, c_h8, 1)
                else:
                    c_h8.n += 1
                    em("scalar", lambda s, qz=qz: qz(s).then_inc(c_h8.h, 1))
                h8_mark[g] = c_h8.n
                if g + 4 < 32:
                    if heng[g % 4] == "sync":
                        em("sync", lambda e, v=h8_mark[g]: e.wait_ge(c_h8.h, v))
                    h8_ld(g + 4)

        def ph_f2():
            h8 = AV(O_H8, FP8, 65536)
            sfm = AV(O_SFM, BF16, 16384)
            tfm = AV(O_TFM, BF16, 16384)
            em("sync", lambda e: e.wait_ge(cc_pj.h, 1))
            th8 = c_h8.n
            em("sync", lambda e, t=th8: e.wait_ge(c_h8.h, t))
            twp = c_qk.n
            em("sync", lambda e, t=twp: e.wait_ge(c_qk.h, t))
            tzr = c_zrun.n
            em("sync", lambda e, t=tzr: e.wait_ge(c_zrun.h, t))
            twpe = c_wpev.n
            em("tensor", lambda x, t=twpe: x.wait_ge(c_wpev.h, t))
            em("tensor", lambda x, t=th8: x.wait_ge(c_h8.h, t))
            for oc in range(16):
                wb = AV(O_W8P0 if oc % 2 == 0 else O_W8P1, FP8, 8192)
                if oc >= 2:
                    v = c_qk.n - 1
                    em("sync", lambda e, v=v: e.wait_ge(c_qk.h, v))
                sp = s_pj[oc % 2]
                def ldw(e, oc=oc, wb=wb, sp=sp):
                    e.dma_start(out=wb, in_=agpj_out[oc * 128:(oc + 1) * 128, :]).then_inc(sp.h, 16)
                em("sync", ldw)
                sp.n += 16
                tw = sp.n
                em("tensor", lambda x, t=tw, sp=sp: x.wait_ge(sp.h, t))
                if oc >= 2:
                    v = c_sc2.n - 1
                    em("tensor", lambda x, v=v: x.wait_ge(c_sc2.h, v))
                wb3 = wb.rearrange("p (c i m) -> p c i m", i=2, m=128)
                h83 = h8.rearrange("p (c i n) -> p c i n", i=2, n=1024)
                for hh in range(32):
                    for j in range(2):
                        last = (hh == 31 and j == 1)
                        def mm(x, oc=oc, hh=hh, j=j, wb3=wb3, h83=h83, last=last):
                            ins = x.matmul(
                                PS[:, (oc % 2) * 1024 + 512 * j:(oc % 2) * 1024 + 512 * (j + 1)],
                                wb3[:, hh],
                                h83[:, hh, :, 512 * j: 512 * (j + 1)],
                                start=(hh == 0), stop=(hh == 31), perf_mode=DRM,
                                skip_group_check=True)
                            if last:
                                ins.then_inc(c_qk.h, 1)
                        em("tensor", mm)
                c_qk.n += 1
                tqk = c_qk.n
                em("scalar", lambda s, t=tqk: s.wait_ge(c_qk.h, t))
                gb = AV(O_GB0 if oc % 2 == 0 else O_GB1, F32, 1024)
                if oc >= 2:
                    v = c_g.n - 1
                    em("scalar", lambda s, v=v: s.wait_ge(c_g.h, v))
                sss(lambda s, oc=oc, gb=gb: s.activation(
                    gb, PS[:, (oc % 2) * 1024:(oc % 2) * 1024 + 1024], AF.Silu,
                    scale=scal_bc[:, 5:6]))
                c_sc2.n += 1
                em("scalar", lambda s: s.activation(srow[0:1, 48:49], srow[0:1, 48:49], AF.Identity).then_inc(c_sc2.h, 1))
                tsl = c_sc2.n
                em("vector", lambda v, t=tsl: v.wait_ge(c_sc2.h, t))
                vsig(lambda v, oc=oc, gb=gb, sfm=sfm, tfm=tfm: v.tensor_mul(
                    tfm[:, oc * 1024:(oc + 1) * 1024],
                    sfm[:, oc * 1024:(oc + 1) * 1024], gb), c_g)
            tg_all = c_g.n
            em("tensor", lambda x, t=tg_all: x.wait_ge(c_g.h, t))
            em("sync", lambda e, t=tzr: e.wait_ge(c_zrun.h, t))
            for t_ in range(2):
                xb = AV(O_XT0 if t_ % 2 == 0 else O_XT1, F32, 2048)
                sxx = s_xtm[t_ % 2]
                def ldx(e, t_=t_, xb=xb, sxx=sxx):
                    e.dma_start(out=xb, in_=x_tm[t_ * 128:(t_ + 1) * 128, :]).then_inc(sxx.h, 16)
                em("sync", ldx)
                sxx.n += 16
            for t_ in range(8):
                xb = AV(O_XT0 if t_ % 2 == 0 else O_XT1, F32, 2048)
                pst2 = PS[:, 2048 + (t_ % 2) * 1024: 2048 + (t_ % 2) * 1024 + 1024].bitcast(BF16)
                if t_ >= 2:
                    v = c_tr.n - 1
                    em("tensor", lambda x, v=v: x.wait_ge(c_tr.h, v))
                for oc in range(16):
                    last = (oc == 15)
                    def tr(x, t_=t_, oc=oc, pst2=pst2, last=last):
                        ins = x.transpose(pst2[:, oc * 128:(oc + 1) * 128],
                                          tfm[:, oc * 1024 + t_ * 128: oc * 1024 + (t_ + 1) * 128],
                                          ident_bf[:, :])
                        if last:
                            ins.then_inc(c_qk.h, 1)
                    em("tensor", tr)
                c_qk.n += 1
                tqk = c_qk.n
                em("vector", lambda v, t=tqk: v.wait_ge(c_qk.h, t))
                sxx = s_xtm[t_ % 2]
                tx = 16 * (t_ // 2 + 1)
                em("vector", lambda v, t=tx, sxx=sxx: v.wait_ge(sxx.h, t))
                ob = AV(O_OUT, F32, 2048)
                if t_ >= 1:
                    v = s_out.n
                    em("vector", lambda w, v=v: w.wait_ge(s_out.h, v))
                vsig(lambda v, xb=xb, ob=ob, pst2=pst2: v.tensor_add(ob, xb, pst2), c_tr)
                ttr = c_tr.n
                em("sync", lambda e, t=ttr: e.wait_ge(c_tr.h, t))
                def sto(e, t_=t_, ob=ob):
                    e.dma_start(out=out[t_ * 128:(t_ + 1) * 128, :], in_=ob).then_inc(s_out.h, 16)
                em("sync", sto)
                s_out.n += 16
                if t_ + 2 < 8:
                    t2 = t_ + 2
                    xb2 = AV(O_XT0 if t2 % 2 == 0 else O_XT1, F32, 2048)
                    sx2p = s_xtm[t2 % 2]
                    def ldx2(e, t2=t2, xb2=xb2, sx2p=sx2p):
                        e.dma_start(out=xb2, in_=x_tm[t2 * 128:(t2 + 1) * 128, :]).then_inc(sx2p.h, 16)
                    em("sync", ldx2)
                    sx2p.n += 16
            tout = s_out.n
            em("sync", lambda e, t=tout: e.wait_ge(s_out.h, t))

        ph_consts()
        ph_wmax()
        ph_lnA()
        ph_q_pre()
        ph_ar1()
        ph_q()
        ph_wq8()
        ph_lnB()
        ph_minit()
        ph_att()
        ph_f1()
        ph_wp_loads(range(0, 2), wait_region=True)
        ph_ar2()
        ph_wp_loads(range(2, 16))
        ph_wp()
        ph_h8()
        ph_f2()

        @blk.sync
        def _(e):
            for f in prog["sync"]:
                f(e)

        @blk.gpsimd
        def _(g):
            for f in prog["gpsimd"]:
                f(g)

        @blk.scalar
        def _(s):
            for f in prog["scalar"]:
                f(s)

        @blk.vector
        def _(v):
            for f in prog["vector"]:
                f(v)

        @blk.tensor
        def _(t):
            for f in prog["tensor"]:
                f(t)

    return nc


def _stage_inputs(L0_attn_query, L0_attn_Wqkv, L0_attn_Wproj, L0_attn_bproj,
                  L0_ffn_w_fc, L0_ffn_w_proj):
    x = np.ascontiguousarray(L0_attn_query, dtype=np.float32)
    Wqkv = np.asarray(L0_attn_Wqkv, dtype=np.float32)
    Wproj = np.asarray(L0_attn_Wproj, dtype=np.float32)
    bproj = np.asarray(L0_attn_bproj, dtype=np.float32)
    w_fc = np.asarray(L0_ffn_w_fc, dtype=np.float32)
    w_proj = np.asarray(L0_ffn_w_proj, dtype=np.float32)

    ident = np.eye(128, dtype=np.float32)
    bpc = np.ascontiguousarray(bproj.reshape(16, 128).T)

    WqkvT = np.ascontiguousarray(Wqkv.T)
    T4 = np.ascontiguousarray(Wproj.T).reshape(16, 128, 16, 128)
    Wr = np.ascontiguousarray(T4.transpose(2, 1, 0, 3)).reshape(2048, 2048)
    F4 = np.ascontiguousarray(w_fc.T).reshape(16, 128, 64, 128)
    WfcR = np.ascontiguousarray(F4.transpose(2, 1, 0, 3)).reshape(8192, 2048)
    P4 = np.ascontiguousarray(w_proj.T).reshape(64, 128, 16, 128)
    WpjR = np.ascontiguousarray(P4.transpose(2, 1, 0, 3)).reshape(2048, 8192)

    in_maps = []
    for i in range(8):
        b_att = i // 2
        hsel = i % 2
        b_ffn = i % 4
        half = i // 4
        in_maps.append({
            "x_tma": np.ascontiguousarray(x[b_att]),
            "x_tm": np.ascontiguousarray(x[b_ffn, half * 1024:(half + 1) * 1024, :]),
            "wqkvT": np.ascontiguousarray(WqkvT[:, 3072 * hsel: 3072 * (hsel + 1)]),
            "wpr": Wr,
            "bproj_cols": bpc,
            "wfc_sl": np.ascontiguousarray(WfcR[1024 * i: 1024 * (i + 1)]),
            "wpj_sl": np.ascontiguousarray(WpjR[256 * i: 256 * (i + 1)]),
            "ident": ident,
        })
    return in_maps


def kernel(L0_attn_query, L0_attn_key, L0_attn_value, L0_attn_Wqkv,
           L0_attn_Wproj, L0_attn_bproj, L0_ffn_w_fc, L0_ffn_w_proj):
    in_maps = _stage_inputs(L0_attn_query, L0_attn_Wqkv, L0_attn_Wproj,
                            L0_attn_bproj, L0_ffn_w_fc, L0_ffn_w_proj)
    nc = _build()
    res = run_bass_kernel_spmd(nc, in_maps, core_ids=list(range(8)))
    o = np.empty((NB, S, H), dtype=np.float32)
    for i in range(8):
        b_ffn = i % 4
        half = i // 4
        o[b_ffn, half * 1024:(half + 1) * 1024, :] = res.results[i]["out"]
    return o




# version 12 (final)
# speedup vs baseline: 1.0378x; 1.0378x over previous
import numpy as np
import concourse.bass as bass
import concourse.mybir as mybir
from concourse.bass_utils import run_bass_kernel_spmd
from concourse.aot_env import allow_ici_all_to_all

F32 = mybir.dt.float32
F32R = mybir.dt.float32r
BF16 = mybir.dt.bfloat16
FP8 = mybir.dt.float8e4
U8 = mybir.dt.uint8
AF = mybir.ActivationFunctionType
DRM = mybir.MatmulPerfMode.DoubleRow
ALU = mybir.AluOpType
AX = mybir.AxisListType

H = 2048
S = 2048
NB = 4
TOK = 1024
NORM = float(1.0 / np.sqrt(128.0))
R448 = 1.0 / 448.0
ARENA = 180224
ESZ = {F32: 4, F32R: 4, BF16: 2, FP8: 1, U8: 1}

O_XNBF = 0
O_QKV = 65536
O_XB0, O_XB1 = 65536, 73728
O_XSQ0, O_XSQ1 = 81920, 90112
O_TB = 98304
O_XNF0, O_XNF1 = 106496, 114688
O_RBC, O_MRBC = 122880, 131072
O_R2BC, O_MR2BC = 139264, 143360
O_X2B0, O_X2B1 = 147456, 151552
O_XN2F0, O_XN2F1 = 155648, 159744
O_WM0, O_WM1 = 122880, 131072
O_WQ0, O_WQ1 = 163840, 167936
O_WSLF0, O_WSLF1 = 172032, 174080
O_WSL80, O_WSL81 = 176128, 176640
O_VTM = 0
O_LB0, O_LB1 = 0, 8192
O_LT0, O_LT1 = 16384, 24576
O_X8 = 32768
O_XNH0, O_XNH1 = 49152, 53248
O_PT = (49152, 51200, 53248, 55296)
O_RDBC = 57344
O_CTXF = 61440
O_CTXB = 172032
O_W8FC0, O_W8FC1 = 65536, 67584
O_ZB0, O_ZB1 = 69632, 73728
O_SFM = 0
O_CTXL = 32768
O_H8 = 65536
O_WPR0, O_WPR1 = 131072, 135168
O_ZL0, O_ZL1 = 139264, 143360
O_TFM = 32768
O_W8P0, O_W8P1 = 131072, 139264
O_GB0, O_GB1 = 147456, 151552
O_XT0, O_XT1 = 155648, 163840
O_OUT = 172032


class Sem:
    __slots__ = ("h", "n")

    def __init__(self, h):
        self.h = h
        self.n = 0


def _build():
    nc = bass.Bass("TRN2", target_bir_lowering=False, debug=False)

    x_tma = nc.dram_tensor("x_tma", [S, H], F32, kind="ExternalInput")
    x_tm = nc.dram_tensor("x_tm", [TOK, H], F32, kind="ExternalInput")
    wqkvT = nc.dram_tensor("wqkvT", [H, 3072], F32, kind="ExternalInput")
    wpr = nc.dram_tensor("wpr", [H, H], F32, kind="ExternalInput")
    bproj_cols = nc.dram_tensor("bproj_cols", [128, 16], F32, kind="ExternalInput")
    wfc_sl = nc.dram_tensor("wfc_sl", [1024, 2048], F32, kind="ExternalInput")
    wpj_sl = nc.dram_tensor("wpj_sl", [256, 8192], F32, kind="ExternalInput")
    ident = nc.dram_tensor("ident", [128, 128], F32, kind="ExternalInput")
    out = nc.dram_tensor("out", [TOK, H], F32, kind="ExternalOutput")

    z_d = nc.dram_tensor("z_d", [8192, TOK], F32)
    a2a_in = nc.dram_tensor("a2a_in", [2048, TOK], BF16)
    a2a_out = nc.dram_tensor("a2a_out", [2048, TOK], BF16)
    agfc_in = nc.dram_tensor("agfc_in", [1024, 2048], FP8)
    agfc_out = nc.dram_tensor("agfc_out", [8192, 2048], FP8, addr_space="Shared")
    agpj_in = nc.dram_tensor("agpj_in", [256, 8192], FP8)
    agpj_out = nc.dram_tensor("agpj_out", [2048, 8192], FP8, addr_space="Shared")
    ar1_in = nc.dram_tensor("ar1_in", [8, 1], F32)
    ar1_out = nc.dram_tensor("ar1_out", [8, 1], F32)
    ar2_in = nc.dram_tensor("ar2_in", [8, 1], F32)
    ar2_out = nc.dram_tensor("ar2_out", [8, 1], F32)
    flat_d = nc.dram_tensor("flat_d", [512], F32)
    drow_d = nc.dram_tensor("drow_d", [8192], F32)

    prog = {k: [] for k in ("sync", "gpsimd", "scalar", "vector", "tensor")}

    def em(eng, fn):
        prog[eng].append(fn)

    def dview(t, off, n):
        return bass.AP(t, off, [[n, 1], [1, n]])

    with (
        nc.sbuf_tensor("arena", [128, ARENA], U8) as arena_t,
        nc.sbuf_tensor("ident_bf", [128, 128], BF16) as ident_bf,
        nc.sbuf_tensor("ones_f", [128, 1], F32) as ones_f,
        nc.sbuf_tensor("ones_b", [128, 1], BF16) as ones_b,
        nc.sbuf_tensor("bpc", [128, 16], F32) as bpc,
        nc.sbuf_tensor("scal_bc", [128, 8], F32) as scal_bc,
        nc.sbuf_tensor("rows", [1, 6144], F32) as rows_t,
        nc.sbuf_tensor("srow", [1, 64], F32) as srow,
        nc.sbuf_tensor("red8", [128, 1], F32) as red8,
        nc.sbuf_tensor("runx", [128, 1], F32) as runx,
        nc.sbuf_tensor("runf", [128, 1], F32) as runf,
        nc.sbuf_tensor("runp", [128, 1], F32) as runp,
        nc.sbuf_tensor("runz", [128, 1], F32) as runz,
        nc.sbuf_tensor("redt", [128, 1], F32) as redt,
        nc.sbuf_tensor("flat3", [4, 128], F32) as flat3,
        nc.sbuf_tensor("ident_f", [128, 128], F32) as ident_f,
        nc.sbuf_tensor("ln_t", [128, 64], F32) as ln_t,
        nc.psum_tensor("PS", [128, 4096], F32) as PS,
        nc.Block() as blk,
    ):
        arena = arena_t.ap()

        def AV(off, dtype, n):
            assert off % 512 == 0 and off + n * ESZ[dtype] <= ARENA, (off, n)
            return arena[:, off: off + n * ESZ[dtype]].bitcast(dtype)

        SN = [Sem(nc.alloc_semaphore(name=f"S{i}")) for i in range(60)]
        (s_misc, s_lx0, s_wm0, s_bc, s_x20, s_xnh0, s_wq0, s_sbc, s_x8ld0, s_wslld0,
         s_rdb, s_ctxst, s_fc0, s_zst0, s_zld0, s_pj0, s_wpr0, s_ctxld, s_xtm0,
         s_out, c_xsq, c_stat, c_misc, c_xn, c_qk, c_qev, c_exp, c_av,
         c_ctxv, c_zrun, c_h8, c_wpev, c_sc2, c_g, c_tr, cc_ar1, cc_fc,
         cc_pj, cc_a2a, cc_ar2, c_ss, c_sss,
         s_lx1, s_wm1, s_x21, s_xnh1, s_wq1, s_x8ld1, s_wslld1, s_fc1,
         s_zst1, s_zld1, s_pj1, s_wpr1, s_xtm1, s_wslst0, s_wslst1,
         _sp0, _sp1, _sp2) = SN
        s_lx = (s_lx0, s_lx1)
        s_wm = (s_wm0, s_wm1)
        s_x2 = (s_x20, s_x21)
        s_xnh = (s_xnh0, s_xnh1)
        s_wq = (s_wq0, s_wq1)
        s_x8ld = (s_x8ld0, s_x8ld1)
        s_wslld = (s_wslld0, s_wslld1)
        s_wslst = (s_wslst0, s_wslst1)
        s_fc = (s_fc0, s_fc1)
        s_zst = (s_zst0, s_zst1)
        s_zld = (s_zld0, s_zld1)
        s_pj = (s_pj0, s_pj1)
        s_wpr = (s_wpr0, s_wpr1)
        s_xtm = (s_xtm0, s_xtm1)

        def vss(fn):
            c_ss.n += 1
            em("vector", lambda v, fn=fn: fn(v).then_inc(c_ss.h, 1))
            em("vector", lambda v, n=c_ss.n: v.wait_ge(c_ss.h, n))

        def sss(fn):
            c_sss.n += 1
            em("scalar", lambda s, fn=fn: fn(s).then_inc(c_sss.h, 1))
            em("scalar", lambda s, n=c_sss.n: s.wait_ge(c_sss.h, n))

        def vsig(fn, sem, k=1):
            vss(fn)
            sem.n += k
            em("vector", lambda v, sem=sem, k=k: v.engine_nop().then_inc(sem.h, k))

        def ssig(fn, sem, k=1):
            sss(fn)
            sem.n += k
            em("scalar", lambda s, sem=sem, k=k: s.activation(srow[0:1, 48:49], srow[0:1, 48:49], AF.Identity).then_inc(sem.h, k))

        rowA = rows_t.ap()[0:1, 0:2048]
        rowB = rows_t.ap()[0:1, 2048:4096]
        rowC = rows_t.ap()[0:1, 4096:6144]

        stt = ln_t.ap()[:, 0:24]
        ag = ln_t.ap()[:, 24:26]
        vt = ln_t.ap()[:, 26:27]
        st0 = ln_t.ap()[:, 27:28]
        rr = ln_t.ap()[:, 28:29]
        nmr = ln_t.ap()[:, 29:30]
        qkv = AV(O_QKV, BF16, 49152)
        xn_bf = AV(O_XNBF, BF16, 32768)

        def ph_consts():
            em("gpsimd", lambda g: g.dma_start(out=ident_bf[:, :], in_=ident[:, :]).then_inc(s_bc.h, 16))
            s_bc.n += 16
            em("gpsimd", lambda g: g.memset(ones_f[:, :], 1.0))
            em("gpsimd", lambda g: g.memset(red8[:, :], 0.0))
            em("gpsimd", lambda g: g.memset(srow[0:1, :], 0.0))
            em("gpsimd", lambda g: g.memset(ones_b[:, :], 1.0).then_inc(c_misc.h, 1))
            c_misc.n += 1
            em("sync", lambda e: e.dma_start(out=bpc[:, :], in_=bproj_cols[:, :]).then_inc(s_misc.h, 16))
            s_misc.n += 16
            em("sync", lambda e: e.dma_start(out=ident_f[:, :], in_=ident[:, :]).then_inc(s_misc.h, 16))
            s_misc.n += 16
            tb = s_bc.n
            tm = s_misc.n
            tc = c_misc.n
            em("tensor", lambda x, t=tb: x.wait_ge(s_bc.h, t))
            em("tensor", lambda x, t=tm: x.wait_ge(s_misc.h, t))
            em("tensor", lambda x, t=tc: x.wait_ge(c_misc.h, t))

        def ph_wmax():
            for t_i, (src, pieces, run) in enumerate(
                    ((wfc_sl, [(c, 0) for c in range(8)], runf),
                     (wpj_sl, [(c, q) for c in range(2) for q in range(4)], runp))):
                for pi, (c, q) in enumerate(pieces):
                    buf = AV(O_WM0 if pi % 2 == 0 else O_WM1, F32, 2048)
                    if not (t_i == 0 and pi < 2):
                        v = c_xsq.n - 1
                        em("sync", lambda e, v=v: e.wait_ge(c_xsq.h, v))
                    sw = s_wm[pi % 2]
                    def ld(e, src=src, c=c, q=q, buf=buf, sw=sw):
                        e.dma_start(out=buf, in_=src[c * 128:(c + 1) * 128, q * 2048:(q + 1) * 2048]).then_inc(sw.h, 16)
                    em("sync", ld)
                    sw.n += 16
                    tw = sw.n
                    em("vector", lambda v, t=tw, sw=sw: v.wait_ge(sw.h, t))
                    first = (pi == 0)
                    if first:
                        vsig(lambda v, buf=buf, run=run: v.reduce_max(run[:, 0:1], buf, axis=AX.X, apply_absolute_value=True), c_xsq)
                    else:
                        vss(lambda v, buf=buf: v.reduce_max(redt[:, 0:1], buf, axis=AX.X, apply_absolute_value=True))
                        vsig(lambda v, run=run: v.tensor_max(run[:, 0:1], run[:, 0:1], redt[:, 0:1]), c_xsq)

        def ph_lnA():
            nm_mark = {}
            ax_mark = {}
            def absmax(c):
                xt = AV(O_TB if c % 2 == 0 else O_XNF0, F32, 2048)
                em("vector", lambda v, t=nm_mark[c]: v.wait_ge(c_misc.h, t))
                if c == 0:
                    vsig(lambda v, xt=xt: v.reduce_max(runx[:, 0:1], xt, axis=AX.X, apply_absolute_value=True), c_xn)
                else:
                    vss(lambda v, xt=xt: v.reduce_max(redt[:, 0:1], xt, axis=AX.X, apply_absolute_value=True))
                    vsig(lambda v: v.tensor_max(runx[:, 0:1], runx[:, 0:1], redt[:, 0:1]), c_xn)
                ax_mark[c] = c_xn.n
            for c in range(16):
                p = 32 * (c % 2)
                sttp = ln_t.ap()[:, p + 0:p + 24]
                agp = ln_t.ap()[:, p + 24:p + 26]
                vtp = ln_t.ap()[:, p + 26:p + 27]
                st0p = ln_t.ap()[:, p + 27:p + 28]
                rrp = ln_t.ap()[:, p + 28:p + 29]
                nmrp = ln_t.ap()[:, p + 29:p + 30]
                xb = AV(O_XB0 if c % 2 == 0 else O_XB1, F32, 2048)
                xt = AV(O_TB if c % 2 == 0 else O_XNF0, F32, 2048)
                sl = s_lx[c % 2]
                if c >= 2:
                    v = ax_mark[c - 2]
                    em("sync", lambda e, v=v: e.wait_ge(c_xn.h, v))
                def ld(e, c=c, xb=xb, sl=sl):
                    e.dma_start(out=xb, in_=x_tma[c * 128:(c + 1) * 128, :]).then_inc(sl.h, 16)
                em("sync", ld)
                sl.n += 16
                tl = sl.n
                em("vector", lambda v, t=tl, sl=sl: v.wait_ge(sl.h, t))
                if c >= 2:
                    v = nm_mark[c - 2]
                    em("vector", lambda w, v=v: w.wait_ge(c_misc.h, v))
                for q in range(3):
                    em("vector", lambda v, q=q, xb=xb, sttp=sttp: v.bn_stats(sttp[:, q * 6:(q + 1) * 6], xb[:, q * 512:(q + 1) * 512]))
                vss(lambda v, xb=xb, sttp=sttp: v.bn_stats(sttp[:, 18:24], xb[:, 1536:2048]))
                vss(lambda v, agp=agp, sttp=sttp: v.bn_aggr(agp[:, :], sttp[:, :]))
                vss(lambda v, vtp=vtp, agp=agp: v.tensor_scalar_add(vtp[:, 0:1], agp[:, 1:2], 1e-5))
                c_xsq.n += 1
                em("vector", lambda v: v.engine_nop().then_inc(c_xsq.h, 1))
                ts0 = c_xsq.n
                em("scalar", lambda s, t=ts0: s.wait_ge(c_xsq.h, t))
                c_sc2.n += 1
                em("scalar", lambda s, st0p=st0p, vtp=vtp: s.activation(st0p[:, 0:1], vtp[:, 0:1], AF.Sqrt).then_inc(c_sc2.h, 1))
                ts1 = c_sc2.n
                em("vector", lambda v, t=ts1: v.wait_ge(c_sc2.h, t))
                vss(lambda v, rrp=rrp, st0p=st0p: v.reciprocal(rrp[:, 0:1], st0p[:, 0:1]))
                vss(lambda v, nmrp=nmrp, rrp=rrp: v.tensor_mul(nmrp[:, 0:1], rrp[:, 0:1], rrp[:, 0:1]))
                vss(lambda v, nmrp=nmrp, vtp=vtp: v.tensor_mul(nmrp[:, 0:1], nmrp[:, 0:1], vtp[:, 0:1]))
                vss(lambda v, nmrp=nmrp: v.tensor_scalar(nmrp[:, 0:1], nmrp[:, 0:1], -0.5, 1.5, ALU.mult, ALU.add))
                vss(lambda v, rrp=rrp, nmrp=nmrp: v.tensor_mul(rrp[:, 0:1], rrp[:, 0:1], nmrp[:, 0:1]))
                vss(lambda v, nmrp=nmrp, agp=agp, rrp=rrp: v.tensor_mul(nmrp[:, 0:1], agp[:, 0:1], rrp[:, 0:1]))
                c_g.n += 1
                em("vector", lambda v, nmrp=nmrp: v.tensor_scalar_mul(nmrp[:, 0:1], nmrp[:, 0:1], -1.0).then_inc(c_g.h, 1))
                tn = c_g.n
                em("scalar", lambda s, t=tn: s.wait_ge(c_g.h, t))
                if c >= 2:
                    v = c_qk.n - 1
                    em("scalar", lambda s, v=v: s.wait_ge(c_qk.h, v))
                if c == 0:
                    em("scalar", lambda s, v=c_misc.n: s.wait_ge(c_misc.h, v))
                c_misc.n += 1
                em("scalar", lambda s, xb=xb, xt=xt, rrp=rrp, nmrp=nmrp: s.activation(
                    xt, xb, AF.Identity, scale=rrp[:, 0:1], bias=nmrp[:, 0:1]).then_inc(c_misc.h, 1))
                tm = c_misc.n
                nm_mark[c] = tm
                if c >= 1:
                    absmax(c - 1)
                em("tensor", lambda x, t=tm: x.wait_ge(c_misc.h, t))
                if c >= 2:
                    v = c_qev.n - 1
                    em("tensor", lambda x, v=v: x.wait_ge(c_qev.h, v))
                pst = PS[:, (c % 2) * 2048:(c % 2) * 2048 + 2048]
                for f in range(16):
                    last = (f == 15)
                    def tr(x, c=c, f=f, xt=xt, pst=pst, last=last):
                        ins = x.transpose(pst[:, f * 128:(f + 1) * 128],
                                          xt[:, f * 128:(f + 1) * 128], ident_f[:, :])
                        if last:
                            ins.then_inc(c_qk.h, 1)
                    em("tensor", tr)
                c_qk.n += 1
                tq = c_qk.n
                em("scalar", lambda s, t=tq: s.wait_ge(c_qk.h, t))
                c_qev.n += 1
                def ev(s, c=c, pst=pst):
                    s.activation(xn_bf.rearrange("p (f n) -> p f n", n=2048)[:, :, c * 128:(c + 1) * 128],
                                 pst.rearrange("p (f n) -> p f n", n=128), AF.Identity).then_inc(c_qev.h, 1)
                em("scalar", ev)
            absmax(15)

        def ph_ar1():
            t0 = c_xn.n
            em("gpsimd", lambda g, t=t0: g.wait_ge(c_xn.h, t))
            em("gpsimd", lambda g: g.dma_start(out=flat_d[0:128], in_=runx[:, 0:1]).then_inc(s_rdb.h, 16))
            em("gpsimd", lambda g: g.dma_start(out=flat_d[128:256], in_=runf[:, 0:1]).then_inc(s_rdb.h, 16))
            em("gpsimd", lambda g: g.dma_start(out=flat_d[256:384], in_=runp[:, 0:1]).then_inc(s_rdb.h, 16))
            s_rdb.n += 48
            t1 = s_rdb.n
            em("gpsimd", lambda g, t=t1: g.wait_ge(s_rdb.h, t))
            em("gpsimd", lambda g: g.dma_start(out=flat3[0:3, :], in_=bass.AP(flat_d, 0, [[128, 3], [1, 128]])).then_inc(s_rdb.h, 16))
            s_rdb.n += 16
            t2 = s_rdb.n
            em("vector", lambda v, t=t2: v.wait_ge(s_rdb.h, t))
            vsig(lambda v: v.reduce_max(red8[0:3, 0:1], flat3[0:3, :], axis=AX.X), c_misc)
            t3 = c_misc.n
            em("gpsimd", lambda g, t=t3: g.wait_ge(c_misc.h, t))
            em("gpsimd", lambda g: g.dma_start(out=ar1_in[:, :], in_=red8[0:8, 0:1]).then_inc(s_rdb.h, 16))
            s_rdb.n += 16
            t4 = s_rdb.n
            em("gpsimd", lambda g, t=t4: g.wait_ge(s_rdb.h, t))
            cc_ar1.n += 1
            em("gpsimd", lambda g: g.collective_compute(
                "AllReduce", ALU.max, replica_groups=[list(range(8))],
                ins=[ar1_in.ap()], outs=[ar1_out.ap()]).then_inc(cc_ar1.h, 1))

        def ph_ar1_fin():
            em("gpsimd", lambda g: g.wait_ge(cc_ar1.h, 1))
            em("gpsimd", lambda g: g.dma_start(out=srow[0:1, 0:8], in_=dview(ar1_out, 0, 8)).then_inc(s_sbc.h, 16))
            s_sbc.n += 16
            t5 = s_sbc.n
            em("vector", lambda v, t=t5: v.wait_ge(s_sbc.h, t))
            vss(lambda v: v.tensor_scalar(srow[0:1, 0:3], srow[0:1, 0:3], R448, 1e-6, ALU.mult, ALU.max))
            vss(lambda v: v.reciprocal(srow[0:1, 4:7], srow[0:1, 0:3]))
            vss(lambda v: v.tensor_scalar_mul(srow[0:1, 4:7], srow[0:1, 4:7], 0.5))
            vss(lambda v: v.tensor_mul(srow[0:1, 7:8], srow[0:1, 0:1], srow[0:1, 1:2]))
            vss(lambda v: v.tensor_scalar_mul(srow[0:1, 7:8], srow[0:1, 7:8], 4.0))
            vss(lambda v: v.tensor_copy(srow[0:1, 16:19], srow[0:1, 4:7]))
            vsig(lambda v: v.tensor_copy(srow[0:1, 19:20], srow[0:1, 7:8]), c_misc)
            t6 = c_misc.n
            em("gpsimd", lambda g, t=t6: g.wait_ge(c_misc.h, t))
            em("gpsimd", lambda g: g.dma_start(out=drow_d[4096:4104], in_=srow[0:1, 16:24]).then_inc(s_sbc.h, 16))
            s_sbc.n += 16
            t7 = s_sbc.n
            em("gpsimd", lambda g, t=t7: g.wait_ge(s_sbc.h, t))
            em("gpsimd", lambda g: g.dma_start(out=scal_bc[:, 0:8], in_=dview(drow_d, 4096, 8).partition_broadcast(128)).then_inc(s_sbc.h, 16))
            s_sbc.n += 16

        def ph_lnB():
            nmb_mark = {}
            x8 = AV(O_X8, FP8, 16384)
            tsb = s_sbc.n
            em("scalar", lambda s, t=tsb: s.wait_ge(s_sbc.h, t))
            for c in range(8):
                xb = AV(O_LB0 if c % 2 == 0 else O_LB1, F32, 2048)
                xt = AV(O_LT0 if c % 2 == 0 else O_LT1, F32, 2048)
                sl = s_x2[c % 2]
                if c < 2:
                    v = c_qev.n
                    em("sync", lambda e, v=v: e.wait_ge(c_qev.h, v))
                else:
                    v = c_misc.n - 1
                    em("sync", lambda e, v=v: e.wait_ge(c_misc.h, v))
                def ld(e, c=c, xb=xb, sl=sl):
                    e.dma_start(out=xb, in_=x_tm[c * 128:(c + 1) * 128, :]).then_inc(sl.h, 16)
                em("sync", ld)
                sl.n += 16
                tl = sl.n
                em("vector", lambda v, t=tl, sl=sl: v.wait_ge(sl.h, t))
                p = 32 * (c % 2)
                sttp = ln_t.ap()[:, p + 0:p + 24]
                agp = ln_t.ap()[:, p + 24:p + 26]
                vtp = ln_t.ap()[:, p + 26:p + 27]
                st0p = ln_t.ap()[:, p + 27:p + 28]
                rrp = ln_t.ap()[:, p + 28:p + 29]
                nmrp = ln_t.ap()[:, p + 29:p + 30]
                if c >= 2:
                    v = nmb_mark[c - 2]
                    em("vector", lambda w, v=v: w.wait_ge(c_misc.h, v))
                for q in range(3):
                    em("vector", lambda v, q=q, xb=xb, sttp=sttp: v.bn_stats(sttp[:, q * 6:(q + 1) * 6], xb[:, q * 512:(q + 1) * 512]))
                vss(lambda v, xb=xb, sttp=sttp: v.bn_stats(sttp[:, 18:24], xb[:, 1536:2048]))
                vss(lambda v, agp=agp, sttp=sttp: v.bn_aggr(agp[:, :], sttp[:, :]))
                vss(lambda v, vtp=vtp, agp=agp: v.tensor_scalar_add(vtp[:, 0:1], agp[:, 1:2], 1e-5))
                c_xsq.n += 1
                em("vector", lambda v: v.engine_nop().then_inc(c_xsq.h, 1))
                ts0 = c_xsq.n
                em("scalar", lambda s, t=ts0: s.wait_ge(c_xsq.h, t))
                c_sc2.n += 1
                em("scalar", lambda s, st0p=st0p, vtp=vtp: s.activation(st0p[:, 0:1], vtp[:, 0:1], AF.Sqrt).then_inc(c_sc2.h, 1))
                ts1 = c_sc2.n
                em("vector", lambda v, t=ts1: v.wait_ge(c_sc2.h, t))
                vss(lambda v, rrp=rrp, st0p=st0p: v.reciprocal(rrp[:, 0:1], st0p[:, 0:1]))
                vss(lambda v, nmrp=nmrp, rrp=rrp: v.tensor_mul(nmrp[:, 0:1], rrp[:, 0:1], rrp[:, 0:1]))
                vss(lambda v, nmrp=nmrp, vtp=vtp: v.tensor_mul(nmrp[:, 0:1], nmrp[:, 0:1], vtp[:, 0:1]))
                vss(lambda v, nmrp=nmrp: v.tensor_scalar(nmrp[:, 0:1], nmrp[:, 0:1], -0.5, 1.5, ALU.mult, ALU.add))
                vss(lambda v, rrp=rrp, nmrp=nmrp: v.tensor_mul(rrp[:, 0:1], rrp[:, 0:1], nmrp[:, 0:1]))
                vss(lambda v, nmrp=nmrp, agp=agp, rrp=rrp: v.tensor_mul(nmrp[:, 0:1], agp[:, 0:1], rrp[:, 0:1]))
                c_g.n += 1
                em("vector", lambda v, nmrp=nmrp: v.tensor_scalar_mul(nmrp[:, 0:1], nmrp[:, 0:1], -1.0).then_inc(c_g.h, 1))
                tn = c_g.n
                em("scalar", lambda s, t=tn: s.wait_ge(c_g.h, t))
                if c >= 2:
                    v = c_qk.n - 1
                    em("scalar", lambda s, v=v: s.wait_ge(c_qk.h, v))
                if c == 0:
                    em("scalar", lambda s, v=c_misc.n: s.wait_ge(c_misc.h, v))
                c_misc.n += 1
                em("scalar", lambda s, xb=xb, xt=xt, rrp=rrp, nmrp=nmrp: s.activation(
                    xt, xb, AF.Identity, scale=rrp[:, 0:1], bias=nmrp[:, 0:1]).then_inc(c_misc.h, 1))
                tm = c_misc.n
                nmb_mark[c] = tm
                em("tensor", lambda x, t=tm: x.wait_ge(c_misc.h, t))
                if c >= 1:
                    v = c_qev.n - 1 + 1
                    em("tensor", lambda x, v=v: x.wait_ge(c_qev.h, v))
                for f in range(16):
                    off = 1024 + (f % 8) * 128 if f < 8 else 3072 + (f % 8) * 128
                    last = (f == 15)
                    def tr(x, c=c, f=f, off=off, xt=xt, last=last):
                        ins = x.transpose(PS[:, off:off + 128],
                                          xt[:, f * 128:(f + 1) * 128], ident_f[:, :])
                        if last:
                            ins.then_inc(c_qk.h, 1)
                    em("tensor", tr)
                c_qk.n += 1
                tq = c_qk.n
                em("scalar", lambda s, t=tq: s.wait_ge(c_qk.h, t))
                x8v = x8.rearrange("p (f n) -> p f n", n=1024)
                def qz(s, c=c, x8v=x8v):
                    s.activation(x8v[:, 0:8, c * 128:(c + 1) * 128],
                                 PS.rearrange("p (f n) -> p f n", n=128)[:, 8:16, :],
                                 AF.Copy, scale=scal_bc[:, 0:1])
                em("scalar", qz)
                c_qev.n += 1
                def qz2(s, c=c, x8v=x8v):
                    s.activation(x8v[:, 8:16, c * 128:(c + 1) * 128],
                                 PS.rearrange("p (f n) -> p f n", n=128)[:, 24:32, :],
                                 AF.Copy, scale=scal_bc[:, 0:1]).then_inc(c_qev.h, 1)
                em("scalar", qz2)

        wq_mark = {}
        qk_mark = {}
        def q_ld(m):
            wb = AV(O_WQ0 if m % 2 == 0 else O_WQ1, BF16, 2048)
            sw = s_wq[m % 2]
            def ldw(g, m=m, wb=wb, sw=sw):
                src = bass.AP(wqkvT, m * 128, [[3072, 128], [3072 * 128, 16], [1, 128]])
                g.dma_start(out=wb, in_=src).then_inc(sw.h, 16)
            em("gpsimd", ldw)
            sw.n += 16
            wq_mark[m] = sw.n
        def ph_q_pre():
            q_ld(0)
            q_ld(1)
        def ph_q():
            for m in range(24):
                wb = AV(O_WQ0 if m % 2 == 0 else O_WQ1, BF16, 2048)
                sw = s_wq[m % 2]
                tw = wq_mark[m]
                em("tensor", lambda x, t=tw, sw=sw: x.wait_ge(sw.h, t))
                if m == 0:
                    tx = c_xn.n
                    em("tensor", lambda x, t=tx: x.wait_ge(c_xn.h, t))
                    tev = c_qev.n
                    em("tensor", lambda x, t=tev: x.wait_ge(c_qev.h, t))
                if m >= 2:
                    v = c_qev.n - 1
                    em("tensor", lambda x, v=v: x.wait_ge(c_qev.h, v))
                for c in range(16):
                    for sh in range(2):
                        for j in range(2):
                            k = 2 * (m % 2) + sh
                            last = (c == 15 and sh == 1 and j == 1)
                            def mm(x, m=m, c=c, sh=sh, j=j, k=k, wb=wb, last=last):
                                ins = x.matmul(
                                    PS[:, 1024 * k + 512 * j: 1024 * k + 512 * (j + 1)],
                                    wb[:, c * 128:(c + 1) * 128],
                                    xn_bf[:, 2048 * c + 1024 * sh + 512 * j: 2048 * c + 1024 * sh + 512 * (j + 1)],
                                    start=(c == 0), stop=(c == 15), skip_group_check=True)
                                if last:
                                    ins.then_inc(c_qk.h, 1)
                            em("tensor", mm)
                c_qk.n += 1
                qk_mark[m] = c_qk.n
                if m + 2 < 24:
                    em("gpsimd", lambda g, v=qk_mark[m]: g.wait_ge(c_qk.h, v))
                    q_ld(m + 2)
                if m == 6:
                    ph_ar1_fin()
                tq = c_qk.n
                em("vector", lambda v, t=tq: v.wait_ge(c_qk.h, t))

                for sh in range(2):
                    k = 2 * (m % 2) + sh
                    last = (sh == 1)
                    def ev(v, m=m, sh=sh, k=k, last=last):
                        ins = v.tensor_copy(qkv[:, 2048 * m + 1024 * sh: 2048 * m + 1024 * (sh + 1)],
                                            PS[:, 1024 * k:1024 * (k + 1)])
                        if last:
                            ins.then_inc(c_ss.h, 1)
                    em("vector", ev)
                c_ss.n += 1
                em("vector", lambda v, n=c_ss.n: v.wait_ge(c_ss.h, n))
                c_qev.n += 1
                em("vector", lambda v: v.engine_nop().then_inc(c_qev.h, 1))

        def ph_wq8():
            tsb = s_sbc.n
            em("scalar", lambda s, t=tsb: s.wait_ge(s_sbc.h, t))
            pieces = []
            for t_i, (src, dst, nch, ncols, scol) in enumerate(
                    ((wfc_sl, agfc_in, 8, 2048, 1), (wpj_sl, agpj_in, 2, 8192, 2))):
                for c in range(nch):
                    for q in range(ncols // 512):
                        pieces.append((src, dst, c, q, scol))
            ld_mark = {}
            qz_mark = {}
            def wq_load(n):
                src, dst, c, q, scol = pieces[n]
                par = n % 2
                fbuf = AV(O_WSLF0 if par == 0 else O_WSLF1, F32, 512)
                slp = s_wslld[par]
                if n >= 2:
                    em("sync", lambda e, v=qz_mark[n - 2]: e.wait_ge(c_sss.h, v))
                def ld(e, src=src, c=c, q=q, fbuf=fbuf, slp=slp):
                    e.dma_start(out=fbuf, in_=src[c * 128:(c + 1) * 128, q * 512:(q + 1) * 512]).then_inc(slp.h, 16)
                em("sync", ld)
                slp.n += 16
                ld_mark[n] = slp.n
            wq_load(0)
            wq_load(1)
            for n, (src, dst, c, q, scol) in enumerate(pieces):
                par = n % 2
                fbuf = AV(O_WSLF0 if par == 0 else O_WSLF1, F32, 512)
                qbuf = AV(O_WSL80 if par == 0 else O_WSL81, FP8, 512)
                if n >= 2:
                    ssp0 = s_wslst[par]
                    v = ssp0.n
                    em("scalar", lambda s, v=v, ssp0=ssp0: s.wait_ge(ssp0.h, v))
                slp = s_wslld[par]
                em("scalar", lambda s, t=ld_mark[n], slp=slp: s.wait_ge(slp.h, t))
                sss(lambda s, fbuf=fbuf, qbuf=qbuf, scol=scol: s.activation(qbuf, fbuf, AF.Copy, scale=scal_bc[:, scol:scol + 1]))
                qz_mark[n] = c_sss.n
                ssp = s_wslst[par]
                def st(s, dst=dst, c=c, q=q, qbuf=qbuf, ssp=ssp):
                    s.dma_start(out=dst[c * 128:(c + 1) * 128, q * 512:(q + 1) * 512], in_=qbuf).then_inc(ssp.h, 16)
                em("scalar", st)
                ssp.n += 16
                if n + 2 < len(pieces):
                    wq_load(n + 2)
            for ssp in s_wslst:
                tall = ssp.n
                em("gpsimd", lambda g, t=tall, ssp=ssp: g.wait_ge(ssp.h, t))
            cc_fc.n += 1
            em("gpsimd", lambda g: g.collective_compute(
                "AllGather", ALU.bypass, replica_groups=[list(range(8))],
                ins=[agfc_in.ap()], outs=[agfc_out.ap()]).then_inc(cc_fc.h, 1))
            cc_pj.n += 1
            em("gpsimd", lambda g: g.collective_compute(
                "AllGather", ALU.bypass, replica_groups=[list(range(8))],
                ins=[agpj_in.ap()], outs=[agpj_out.ap()]).then_inc(cc_pj.h, 1))

        def ph_minit():
            tqe = c_qev.n - 8
            em("tensor", lambda x, t=tqe: x.wait_ge(c_qev.h, t))
            tmc = c_misc.n
            em("vector", lambda v, t=tmc: v.wait_ge(c_misc.h, t))
            v_tm = AV(O_VTM, BF16, 16384)
            for l in range(8):
                m = 3 * l + 2
                pst = PS[:, (l % 2) * 2048:(l % 2) * 2048 + 1024].bitcast(BF16)
                if l >= 2:
                    v = c_av.n - 1
                    em("tensor", lambda x, v=v: x.wait_ge(c_av.h, v))
                for kc in range(16):
                    last = (kc == 15)
                    def tr(x, m=m, kc=kc, pst=pst, last=last):
                        ins = x.transpose(pst[:, kc * 128:(kc + 1) * 128],
                                          qkv[:, 2048 * m + kc * 128: 2048 * m + (kc + 1) * 128],
                                          ident_bf[:, :])
                        if last:
                            ins.then_inc(c_qk.h, 1)
                    em("tensor", tr)
                c_qk.n += 1
                tq = c_qk.n
                em("vector", lambda v, t=tq: v.wait_ge(c_qk.h, t))
                c_av.n += 1
                em("vector", lambda v, l=l, pst=pst, v_tm=v_tm: v.tensor_copy(
                    v_tm[:, l * 2048:(l + 1) * 2048], pst).then_inc(c_av.h, 1))

        def ph_att():
            v_tm = AV(O_VTM, BF16, 16384)
            tvt = c_av.n
            em("tensor", lambda x, t=tvt: x.wait_ge(c_av.h, t))
            em("tensor", lambda x, t=c_qev.n: x.wait_ge(c_qev.h, t))
            d_row = rowC[:, 0:1024]
            for l in range(8):
                for qh in range(2):
                    base_exp = c_exp.n
                    base_av = c_av.n
                    base_qk = c_qk.n
                    if (l, qh) != (0, 0):
                        v = c_ctxv.n
                        em("tensor", lambda x, v=v: x.wait_ge(c_ctxv.h, v))

                    def av_pair(r2):
                        for kd in (2 * r2, 2 * r2 + 1):
                            pt = AV(O_PT[kd % 4], BF16, 1024)
                            for j in range(2):
                                def avmm(x, l=l, kd=kd, j=j, pt=pt):
                                    x.matmul(PS[:, 2048 + 512 * j: 2048 + 512 * (j + 1)],
                                             v_tm[:, 2048 * l + kd * 128: 2048 * l + (kd + 1) * 128],
                                             pt[:, 512 * j: 512 * (j + 1)],
                                             start=(kd == 0), stop=(kd == 15), skip_group_check=True)
                                em("tensor", avmm)
                            for j in range(2):
                                last = (kd % 2 == 1 and j == 1)
                                def dmm(x, kd=kd, j=j, pt=pt, last=last):
                                    ins = x.matmul(PS[0:1, 3072 + 512 * j: 3072 + 512 * (j + 1)],
                                                   ones_b[:, 0:1],
                                                   pt[:, 512 * j: 512 * (j + 1)],
                                                   start=(kd == 0), stop=(kd == 15), skip_group_check=True)
                                    if last:
                                        ins.then_inc(c_av.h, 1)
                                em("tensor", dmm)
                        c_av.n += 1

                    for r in range(8):
                        if r >= 1:
                            v = base_exp + 2 * r
                            em("tensor", lambda x, v=v: x.wait_ge(c_exp.h, v))
                        for kc in (2 * r, 2 * r + 1):
                            for j in range(2):
                                last = (kc % 2 == 1 and j == 1)
                                def qkmm(x, l=l, qh=qh, kc=kc, j=j, last=last):
                                    ins = x.matmul(
                                        PS[:, (kc % 2) * 1024 + 512 * j:(kc % 2) * 1024 + 512 * (j + 1)],
                                        qkv[:, 2048 * (3 * l + 1) + kc * 128: 2048 * (3 * l + 1) + (kc + 1) * 128],
                                        qkv[:, 2048 * (3 * l) + 1024 * qh + 512 * j: 2048 * (3 * l) + 1024 * qh + 512 * (j + 1)],
                                        start=True, stop=True, skip_group_check=True)
                                    if last:
                                        ins.then_inc(c_qk.h, 1)
                                em("tensor", qkmm)
                        c_qk.n += 1
                        if r >= 1:
                            av_pair(r - 1)
                    v = base_exp + 16
                    em("tensor", lambda x, v=v: x.wait_ge(c_exp.h, v))
                    av_pair(7)
                    for r in range(8):
                        v = base_qk + r + 1
                        em("scalar", lambda s, v=v: s.wait_ge(c_qk.h, v))
                        if r >= 2:
                            v2 = base_av + r - 1
                            em("scalar", lambda s, v=v2: s.wait_ge(c_av.h, v))
                        for kc in (2 * r, 2 * r + 1):
                            pt = AV(O_PT[kc % 4], BF16, 1024)
                            c_exp.n += 1
                            em("scalar", lambda s, kc=kc, pt=pt: s.activation(
                                pt, PS[:, (kc % 2) * 1024:(kc % 2) * 1024 + 1024], AF.Exp,
                                scale=NORM).then_inc(c_exp.h, 1))
                    tav = c_av.n
                    em("vector", lambda v, t=tav: v.wait_ge(c_av.h, t))
                    ctxf = AV(O_CTXF, F32, 1024)
                    if (l, qh) != (0, 0):
                        tcs = s_ctxst.n
                        em("vector", lambda v, t=tcs: v.wait_ge(s_ctxst.h, t))
                    em("vector", lambda v, ctxf=ctxf: v.tensor_copy(ctxf, PS[:, 2048:3072]))
                    vss(lambda v: v.tensor_copy(d_row, PS[0:1, 3072:4096]))
                    c_ctxv.n += 1
                    em("vector", lambda v: v.engine_nop().then_inc(c_ctxv.h, 1))
                    if (l, qh) == (0, 0):
                        em("vector", lambda v, t=c_misc.n: v.wait_ge(c_misc.h, t))
                    vsig(lambda v: v.reciprocal(d_row, d_row), c_misc)
                    tcv = c_misc.n
                    em("gpsimd", lambda g, t=tcv: g.wait_ge(c_misc.h, t))
                    em("gpsimd", lambda g: g.dma_start(out=drow_d[6656:7680], in_=d_row).then_inc(s_rdb.h, 16))
                    s_rdb.n += 16
                    tr1 = s_rdb.n
                    em("gpsimd", lambda g, t=tr1: g.wait_ge(s_rdb.h, t))
                    rdbc = AV(O_RDBC, F32, 1024)
                    em("gpsimd", lambda g, rdbc=rdbc: g.dma_start(out=rdbc, in_=dview(drow_d, 6656, 1024).partition_broadcast(128)).then_inc(s_rdb.h, 16))
                    s_rdb.n += 16
                    tr2 = s_rdb.n
                    em("vector", lambda v, t=tr2: v.wait_ge(s_rdb.h, t))
                    ctxb = AV(O_CTXB, BF16, 1024)
                    vsig(lambda v, ctxf=ctxf, rdbc=rdbc, ctxb=ctxb: v.tensor_mul(ctxb, ctxf, rdbc), c_misc)
                    tcb = c_misc.n
                    em("sync", lambda e, t=tcb: e.wait_ge(c_misc.h, t))
                    jdst = 4 * qh + (l % 4)
                    roff = 256 * jdst + 128 * (l // 4)
                    def stc(e, roff=roff, ctxb=ctxb):
                        e.dma_start(out=a2a_in[roff:roff + 128, :], in_=ctxb).then_inc(s_ctxst.h, 16)
                    em("sync", stc)
                    s_ctxst.n += 16
            tall = s_ctxst.n
            em("gpsimd", lambda g, t=tall: g.wait_ge(s_ctxst.h, t))
            cc_a2a.n += 1
            def a2a(g):
                with allow_ici_all_to_all():
                    g.collective_compute("AllToAll", ALU.bypass,
                                         replica_groups=[list(range(8))],
                                         ins=[a2a_in.ap()], outs=[a2a_out.ap()]).then_inc(cc_a2a.h, 1)
            em("gpsimd", a2a)

        def ph_f1():
            x8 = AV(O_X8, FP8, 16384)
            em("sync", lambda e: e.wait_ge(cc_fc.h, 1))
            tq = c_av.n
            em("sync", lambda e, t=tq: e.wait_ge(c_av.h, t))
            em("tensor", lambda x, t=tq: x.wait_ge(c_av.h, t))
            tcx = c_ctxv.n
            em("tensor", lambda x, t=tcx: x.wait_ge(c_ctxv.h, t))
            tx8 = c_qev.n
            em("tensor", lambda x, t=tx8: x.wait_ge(c_qev.h, t))
            for hh in range(64):
                wb = AV(O_W8FC0 if hh % 2 == 0 else O_W8FC1, FP8, 2048)
                if hh >= 2:
                    v = c_qk.n - 1
                    em("sync", lambda e, v=v: e.wait_ge(c_qk.h, v))
                sf = s_fc[hh % 2]
                def ldw(e, hh=hh, wb=wb, sf=sf):
                    e.dma_start(out=wb, in_=agfc_out[hh * 128:(hh + 1) * 128, :]).then_inc(sf.h, 16)
                em("sync", ldw)
                sf.n += 16
                tw = sf.n
                em("tensor", lambda x, t=tw, sf=sf: x.wait_ge(sf.h, t))
                if hh >= 2:
                    v = c_zrun.n - 1
                    em("tensor", lambda x, v=v: x.wait_ge(c_zrun.h, v))
                wb3 = wb.rearrange("p (c i m) -> p c i m", i=2, m=128)
                x83 = x8.rearrange("p (c i n) -> p c i n", i=2, n=1024)
                for c in range(8):
                    for j in range(2):
                        last = (c == 7 and j == 1)
                        def mm(x, hh=hh, c=c, j=j, wb3=wb3, x83=x83, last=last):
                            ins = x.matmul(
                                PS[:, (hh % 2) * 1024 + 512 * j:(hh % 2) * 1024 + 512 * (j + 1)],
                                wb3[:, c],
                                x83[:, c, :, 512 * j: 512 * (j + 1)],
                                start=(c == 0), stop=(c == 7), perf_mode=DRM,
                                skip_group_check=True)
                            if last:
                                ins.then_inc(c_qk.h, 1)
                        em("tensor", mm)
                c_qk.n += 1
                tqk = c_qk.n
                em("scalar", lambda s, t=tqk: s.wait_ge(c_qk.h, t))
                zb = AV(O_ZB0 if hh % 2 == 0 else O_ZB1, F32, 1024)
                if hh >= 2:
                    szp = s_zst[hh % 2]
                    v = szp.n
                    em("scalar", lambda s, v=v, szp=szp: s.wait_ge(szp.h, v))
                    v2 = c_zrun.n - 1
                    em("scalar", lambda s, v=v2: s.wait_ge(c_zrun.h, v))
                if hh == 0:
                    em("scalar", lambda s, v=c_misc.n: s.wait_ge(c_misc.h, v))
                c_misc.n += 1
                em("scalar", lambda s, hh=hh, zb=zb: s.activation(
                    zb, PS[:, (hh % 2) * 1024:(hh % 2) * 1024 + 1024], AF.Relu,
                    scale=scal_bc[:, 3:4]).then_inc(c_misc.h, 1))
                tmw = c_misc.n
                em("scalar", lambda s, t=tmw: s.wait_ge(c_misc.h, t))
                szp = s_zst[hh % 2]
                def st(s, hh=hh, zb=zb, szp=szp):
                    s.dma_start(out=z_d[hh * 128:(hh + 1) * 128, :], in_=zb).then_inc(szp.h, 16)
                em("scalar", st)
                szp.n += 16
                tmv = szp.n
                em("vector", lambda v, t=tmv, szp=szp: v.wait_ge(szp.h, t))
                if hh == 0:
                    vsig(lambda v, zb=zb: v.reduce_max(runz[:, 0:1], zb, axis=AX.X), c_zrun)
                else:
                    vss(lambda v, zb=zb: v.reduce_max(redt[:, 0:1], zb, axis=AX.X))
                    vsig(lambda v: v.tensor_max(runz[:, 0:1], runz[:, 0:1], redt[:, 0:1]), c_zrun)

        def ph_ar2():
            t0 = c_zrun.n
            em("gpsimd", lambda g, t=t0: g.wait_ge(c_zrun.h, t))
            em("gpsimd", lambda g: g.dma_start(out=flat_d[0:128], in_=runz[:, 0:1]).then_inc(s_rdb.h, 16))
            s_rdb.n += 16
            t1 = s_rdb.n
            em("gpsimd", lambda g, t=t1: g.wait_ge(s_rdb.h, t))
            em("gpsimd", lambda g: g.dma_start(out=flat3[0:1, :], in_=dview(flat_d, 0, 128)).then_inc(s_rdb.h, 16))
            s_rdb.n += 16
            t2 = s_rdb.n
            em("vector", lambda v, t=t2: v.wait_ge(s_rdb.h, t))
            vss(lambda v: v.reduce_max(red8[0:1, 0:1], flat3[0:1, :], axis=AX.X))
            em("vector", lambda v, t=c_misc.n: v.wait_ge(c_misc.h, t))
            vsig(lambda v: v.tensor_mul(red8[0:1, 0:1], red8[0:1, 0:1], red8[0:1, 0:1]), c_misc)
            t3 = c_misc.n
            em("gpsimd", lambda g, t=t3: g.wait_ge(c_misc.h, t))
            em("gpsimd", lambda g: g.dma_start(out=ar2_in[:, :], in_=red8[0:8, 0:1]).then_inc(s_rdb.h, 16))
            s_rdb.n += 16
            t4 = s_rdb.n
            em("gpsimd", lambda g, t=t4: g.wait_ge(s_rdb.h, t))
            cc_ar2.n += 1
            em("gpsimd", lambda g: g.collective_compute(
                "AllReduce", ALU.max, replica_groups=[list(range(8))],
                ins=[ar2_in.ap()], outs=[ar2_out.ap()]).then_inc(cc_ar2.h, 1))
            em("gpsimd", lambda g: g.wait_ge(cc_ar2.h, 1))
            em("gpsimd", lambda g: g.dma_start(out=srow[0:1, 32:40], in_=dview(ar2_out, 0, 8)).then_inc(s_sbc.h, 16))
            s_sbc.n += 16
            t5 = s_sbc.n
            em("vector", lambda v, t=t5: v.wait_ge(s_sbc.h, t))
            vss(lambda v: v.tensor_scalar(srow[0:1, 32:33], srow[0:1, 32:33], R448, 1e-6, ALU.mult, ALU.max))
            vss(lambda v: v.reciprocal(srow[0:1, 33:34], srow[0:1, 32:33]))
            vss(lambda v: v.tensor_scalar_mul(srow[0:1, 33:34], srow[0:1, 33:34], 0.5))
            vss(lambda v: v.tensor_mul(srow[0:1, 34:35], srow[0:1, 32:33], srow[0:1, 2:3]))
            c_misc.n += 1
            em("vector", lambda v: v.tensor_scalar_mul(srow[0:1, 34:35], srow[0:1, 34:35], 4.0).then_inc(c_misc.h, 1))
            c_ss.n += 1
            em("vector", lambda v: v.engine_nop().then_inc(c_ss.h, 1))
            em("vector", lambda v, n=c_ss.n: v.wait_ge(c_ss.h, n))
            t6 = c_misc.n
            em("scalar", lambda s, t=t6: s.wait_ge(c_misc.h, t))
            c_sc2.n += 1
            em("scalar", lambda s: s.activation(srow[0:1, 35:36], srow[0:1, 33:34], AF.Sqrt).then_inc(c_sc2.h, 1))
            t7 = c_sc2.n
            em("vector", lambda v, t=t7: v.wait_ge(c_sc2.h, t))
            for _ in range(2):
                vss(lambda v: v.reciprocal(srow[0:1, 36:37], srow[0:1, 35:36]))
                vss(lambda v: v.tensor_mul(srow[0:1, 36:37], srow[0:1, 36:37], srow[0:1, 33:34]))
                vss(lambda v: v.tensor_add(srow[0:1, 35:36], srow[0:1, 35:36], srow[0:1, 36:37]))
                vss(lambda v: v.tensor_scalar_mul(srow[0:1, 35:36], srow[0:1, 35:36], 0.5))
            vss(lambda v: v.tensor_copy(srow[0:1, 20:21], srow[0:1, 35:36]))
            vsig(lambda v: v.tensor_copy(srow[0:1, 21:22], srow[0:1, 34:35]), c_misc)
            t8 = c_misc.n
            em("gpsimd", lambda g, t=t8: g.wait_ge(c_misc.h, t))
            em("gpsimd", lambda g: g.dma_start(out=drow_d[4352:4354], in_=srow[0:1, 20:22]).then_inc(s_sbc.h, 16))
            s_sbc.n += 16
            t9 = s_sbc.n
            em("gpsimd", lambda g, t=t9: g.wait_ge(s_sbc.h, t))
            em("gpsimd", lambda g: g.dma_start(out=scal_bc[:, 4:6], in_=dview(drow_d, 4352, 2).partition_broadcast(128)).then_inc(s_sbc.h, 16))
            s_sbc.n += 16

        wpr_tgt = {}
        def ph_wp_loads(ocs, wait_region=False):
            if wait_region:
                em("gpsimd", lambda g, t=c_xn.n: g.wait_ge(c_xn.h, t))
            for oc in ocs:
                wb = AV(O_WPR0 if oc % 2 == 0 else O_WPR1, BF16, 2048)
                if oc >= 2:
                    v = c_wpev.n + (oc - 2) + 1
                    em("gpsimd", lambda g, v=v: g.wait_ge(c_wpev.h, v))
                sw = s_wpr[oc % 2]
                def ldw(g, oc=oc, wb=wb, sw=sw):
                    g.dma_start(out=wb, in_=wpr[oc * 128:(oc + 1) * 128, :]).then_inc(sw.h, 16)
                em("gpsimd", ldw)
                sw.n += 16
                wpr_tgt[oc] = sw.n

        def ph_wp():
            ctxl = AV(O_CTXL, BF16, 16384)
            sfm = AV(O_SFM, BF16, 16384)
            em("sync", lambda e: e.wait_ge(cc_a2a.h, 1))
            tf1 = c_qk.n
            em("sync", lambda e, t=tf1: e.wait_ge(c_qk.h, t))
            for ih in range(16):
                roff = 256 * (ih // 2) + 128 * (ih % 2)
                def ldc(e, ih=ih, roff=roff, ctxl=ctxl):
                    e.dma_start(out=ctxl[:, ih * 1024:(ih + 1) * 1024], in_=a2a_out[roff:roff + 128, :]).then_inc(s_ctxld.h, 16)
                em("sync", ldc)
                s_ctxld.n += 16
            tzr = c_zrun.n
            em("tensor", lambda x, t=tzr: x.wait_ge(c_zrun.h, t))
            for oc in range(16):
                wb = AV(O_WPR0 if oc % 2 == 0 else O_WPR1, BF16, 2048)
                sw = s_wpr[oc % 2]
                tw = wpr_tgt[oc]
                tc = s_ctxld.n
                em("tensor", lambda x, t=tw, sw=sw: x.wait_ge(sw.h, t))
                if oc == 0:
                    em("tensor", lambda x, t=tc: x.wait_ge(s_ctxld.h, t))
                if oc >= 2:
                    v = c_wpev.n - 1
                    em("tensor", lambda x, v=v: x.wait_ge(c_wpev.h, v))
                for fc in range(16):
                    for j in range(2):
                        last = (fc == 15 and j == 1)
                        def mm(x, oc=oc, fc=fc, j=j, wb=wb, ctxl=ctxl, last=last):
                            ins = x.matmul(
                                PS[:, (oc % 2) * 1024 + 512 * j:(oc % 2) * 1024 + 512 * (j + 1)],
                                wb[:, fc * 128:(fc + 1) * 128],
                                ctxl[:, 1024 * fc + 512 * j: 1024 * fc + 512 * (j + 1)],
                                start=(fc == 0), stop=(fc == 15), skip_group_check=True)
                            if last:
                                ins.then_inc(c_qk.h, 1)
                        em("tensor", mm)
                c_qk.n += 1
                tqk = c_qk.n
                em("scalar", lambda s, t=tqk: s.wait_ge(c_qk.h, t))
                c_wpev.n += 1
                em("scalar", lambda s, oc=oc, sfm=sfm: s.activation(
                    sfm[:, oc * 1024:(oc + 1) * 1024],
                    PS[:, (oc % 2) * 1024:(oc % 2) * 1024 + 1024],
                    AF.Identity, bias=bpc[:, oc:oc + 1]).then_inc(c_wpev.h, 1))

        O_ZL4A, O_ZL4B = 147456, 163840
        def ph_h8():
            h8 = AV(O_H8, FP8, 65536)
            HSLOT = (O_ZL4A, O_ZL4B, O_ZL4A + 8192, 139264)
            hsem = (s_zld0, s_zld1, s_lx0, s_zst0)
            heng = ("sync", "scalar", "sync", "scalar")
            h8_mark = {}
            hl_mark = {}
            def h8_ld(g):
                zb = AV(HSLOT[g % 4], F32, 2048)
                sl = hsem[g % 4]
                def ld(e, g=g, zb=zb, sl=sl):
                    e.dma_start(out=zb, in_=bass.AP(z_d, g * 2 * 128 * 1024,
                                                    [[1024, 128], [128 * 1024, 2], [1, 1024]])).then_inc(sl.h, 16)
                em(heng[g % 4], ld)
                sl.n += 16
                hl_mark[g] = sl.n
            for szp in s_zst:
                tz = szp.n
                em("sync", lambda e, t=tz, szp=szp: e.wait_ge(szp.h, t))
                em("scalar", lambda s, t=tz, szp=szp: s.wait_ge(szp.h, t))
            for g in range(4):
                h8_ld(g)
            tq = s_sbc.n
            em("scalar", lambda s, t=tq: s.wait_ge(s_sbc.h, t))
            for g in range(32):
                zb = AV(HSLOT[g % 4], F32, 2048)
                tl = hl_mark[g]
                sl = hsem[g % 4]
                em("scalar", lambda s, t=tl, sl=sl: s.wait_ge(sl.h, t))
                qz = lambda s, g=g, zb=zb, h8=h8: s.activation(
                    h8[:, g * 2048:(g + 1) * 2048], zb, AF.Square,
                    scale=scal_bc[:, 4:5])
                if g == 31:
                    ssig(qz, c_h8, k=2)
                elif g + 4 < 32 and heng[g % 4] == "scalar":
                    ssig(qz, c_h8, 1)
                else:
                    c_h8.n += 1
                    em("scalar", lambda s, qz=qz: qz(s).then_inc(c_h8.h, 1))
                h8_mark[g] = c_h8.n
                if g + 4 < 32:
                    if heng[g % 4] == "sync":
                        em("sync", lambda e, v=h8_mark[g]: e.wait_ge(c_h8.h, v))
                    h8_ld(g + 4)

        def ph_f2():
            h8 = AV(O_H8, FP8, 65536)
            sfm = AV(O_SFM, BF16, 16384)
            tfm = AV(O_TFM, BF16, 16384)
            em("sync", lambda e: e.wait_ge(cc_pj.h, 1))
            th8 = c_h8.n
            em("sync", lambda e, t=th8: e.wait_ge(c_h8.h, t))
            twp = c_qk.n
            em("sync", lambda e, t=twp: e.wait_ge(c_qk.h, t))
            tzr = c_zrun.n
            em("sync", lambda e, t=tzr: e.wait_ge(c_zrun.h, t))
            twpe = c_wpev.n
            em("tensor", lambda x, t=twpe: x.wait_ge(c_wpev.h, t))
            em("tensor", lambda x, t=th8: x.wait_ge(c_h8.h, t))
            for oc in range(16):
                wb = AV(O_W8P0 if oc % 2 == 0 else O_W8P1, FP8, 8192)
                if oc >= 2:
                    v = c_qk.n - 1
                    em("sync", lambda e, v=v: e.wait_ge(c_qk.h, v))
                sp = s_pj[oc % 2]
                def ldw(e, oc=oc, wb=wb, sp=sp):
                    e.dma_start(out=wb, in_=agpj_out[oc * 128:(oc + 1) * 128, :]).then_inc(sp.h, 16)
                em("sync", ldw)
                sp.n += 16
                tw = sp.n
                em("tensor", lambda x, t=tw, sp=sp: x.wait_ge(sp.h, t))
                if oc >= 2:
                    v = c_sc2.n - 1
                    em("tensor", lambda x, v=v: x.wait_ge(c_sc2.h, v))
                wb3 = wb.rearrange("p (c i m) -> p c i m", i=2, m=128)
                h83 = h8.rearrange("p (c i n) -> p c i n", i=2, n=1024)
                for hh in range(32):
                    for j in range(2):
                        last = (hh == 31 and j == 1)
                        def mm(x, oc=oc, hh=hh, j=j, wb3=wb3, h83=h83, last=last):
                            ins = x.matmul(
                                PS[:, (oc % 2) * 1024 + 512 * j:(oc % 2) * 1024 + 512 * (j + 1)],
                                wb3[:, hh],
                                h83[:, hh, :, 512 * j: 512 * (j + 1)],
                                start=(hh == 0), stop=(hh == 31), perf_mode=DRM,
                                skip_group_check=True)
                            if last:
                                ins.then_inc(c_qk.h, 1)
                        em("tensor", mm)
                c_qk.n += 1
                tqk = c_qk.n
                em("scalar", lambda s, t=tqk: s.wait_ge(c_qk.h, t))
                gb = AV(O_GB0 if oc % 2 == 0 else O_GB1, F32, 1024)
                if oc >= 2:
                    v = c_g.n - 1
                    em("scalar", lambda s, v=v: s.wait_ge(c_g.h, v))
                sss(lambda s, oc=oc, gb=gb: s.activation(
                    gb, PS[:, (oc % 2) * 1024:(oc % 2) * 1024 + 1024], AF.Silu,
                    scale=scal_bc[:, 5:6]))
                c_sc2.n += 1
                em("scalar", lambda s: s.activation(srow[0:1, 48:49], srow[0:1, 48:49], AF.Identity).then_inc(c_sc2.h, 1))
                tsl = c_sc2.n
                em("vector", lambda v, t=tsl: v.wait_ge(c_sc2.h, t))
                vsig(lambda v, oc=oc, gb=gb, sfm=sfm, tfm=tfm: v.tensor_mul(
                    tfm[:, oc * 1024:(oc + 1) * 1024],
                    sfm[:, oc * 1024:(oc + 1) * 1024], gb), c_g)
            tg_all = c_g.n
            em("tensor", lambda x, t=tg_all: x.wait_ge(c_g.h, t))
            em("sync", lambda e, t=tzr: e.wait_ge(c_zrun.h, t))
            for t_ in range(2):
                xb = AV(O_XT0 if t_ % 2 == 0 else O_XT1, F32, 2048)
                sxx = s_xtm[t_ % 2]
                def ldx(e, t_=t_, xb=xb, sxx=sxx):
                    e.dma_start(out=xb, in_=x_tm[t_ * 128:(t_ + 1) * 128, :]).then_inc(sxx.h, 16)
                em("sync", ldx)
                sxx.n += 16
            for t_ in range(8):
                xb = AV(O_XT0 if t_ % 2 == 0 else O_XT1, F32, 2048)
                pst2 = PS[:, 2048 + (t_ % 2) * 1024: 2048 + (t_ % 2) * 1024 + 1024].bitcast(BF16)
                if t_ >= 2:
                    v = c_tr.n - 1
                    em("tensor", lambda x, v=v: x.wait_ge(c_tr.h, v))
                for oc in range(16):
                    last = (oc == 15)
                    def tr(x, t_=t_, oc=oc, pst2=pst2, last=last):
                        ins = x.transpose(pst2[:, oc * 128:(oc + 1) * 128],
                                          tfm[:, oc * 1024 + t_ * 128: oc * 1024 + (t_ + 1) * 128],
                                          ident_bf[:, :])
                        if last:
                            ins.then_inc(c_qk.h, 1)
                    em("tensor", tr)
                c_qk.n += 1
                tqk = c_qk.n
                em("vector", lambda v, t=tqk: v.wait_ge(c_qk.h, t))
                sxx = s_xtm[t_ % 2]
                tx = 16 * (t_ // 2 + 1)
                em("vector", lambda v, t=tx, sxx=sxx: v.wait_ge(sxx.h, t))
                ob = AV(O_OUT, F32, 2048)
                if t_ >= 1:
                    v = s_out.n
                    em("vector", lambda w, v=v: w.wait_ge(s_out.h, v))
                vsig(lambda v, xb=xb, ob=ob, pst2=pst2: v.tensor_add(ob, xb, pst2), c_tr)
                ttr = c_tr.n
                em("sync", lambda e, t=ttr: e.wait_ge(c_tr.h, t))
                def sto(e, t_=t_, ob=ob):
                    e.dma_start(out=out[t_ * 128:(t_ + 1) * 128, :], in_=ob).then_inc(s_out.h, 16)
                em("sync", sto)
                s_out.n += 16
                if t_ + 2 < 8:
                    t2 = t_ + 2
                    xb2 = AV(O_XT0 if t2 % 2 == 0 else O_XT1, F32, 2048)
                    sx2p = s_xtm[t2 % 2]
                    def ldx2(e, t2=t2, xb2=xb2, sx2p=sx2p):
                        e.dma_start(out=xb2, in_=x_tm[t2 * 128:(t2 + 1) * 128, :]).then_inc(sx2p.h, 16)
                    em("sync", ldx2)
                    sx2p.n += 16
            tout = s_out.n
            em("sync", lambda e, t=tout: e.wait_ge(s_out.h, t))

        ph_consts()
        ph_wmax()
        ph_lnA()
        ph_q_pre()
        ph_ar1()
        ph_q()
        ph_wq8()
        ph_lnB()
        ph_minit()
        ph_att()
        ph_f1()
        ph_wp_loads(range(0, 2), wait_region=True)
        ph_ar2()
        ph_wp_loads(range(2, 16))
        ph_wp()
        ph_h8()
        ph_f2()

        @blk.sync
        def _(e):
            for f in prog["sync"]:
                f(e)

        @blk.gpsimd
        def _(g):
            for f in prog["gpsimd"]:
                f(g)

        @blk.scalar
        def _(s):
            for f in prog["scalar"]:
                f(s)

        @blk.vector
        def _(v):
            for f in prog["vector"]:
                f(v)

        @blk.tensor
        def _(t):
            for f in prog["tensor"]:
                f(t)

    return nc


def _stage_inputs(L0_attn_query, L0_attn_Wqkv, L0_attn_Wproj, L0_attn_bproj,
                  L0_ffn_w_fc, L0_ffn_w_proj):
    x = np.ascontiguousarray(L0_attn_query, dtype=np.float32)
    Wqkv = np.asarray(L0_attn_Wqkv, dtype=np.float32)
    Wproj = np.asarray(L0_attn_Wproj, dtype=np.float32)
    bproj = np.asarray(L0_attn_bproj, dtype=np.float32)
    w_fc = np.asarray(L0_ffn_w_fc, dtype=np.float32)
    w_proj = np.asarray(L0_ffn_w_proj, dtype=np.float32)

    ident = np.eye(128, dtype=np.float32)
    bpc = np.ascontiguousarray(bproj.reshape(16, 128).T)

    WqkvT = np.ascontiguousarray(Wqkv.T)
    T4 = np.ascontiguousarray(Wproj.T).reshape(16, 128, 16, 128)
    Wr = np.ascontiguousarray(T4.transpose(2, 1, 0, 3)).reshape(2048, 2048)
    F4 = np.ascontiguousarray(w_fc.T).reshape(16, 128, 64, 128)
    WfcR = np.ascontiguousarray(F4.transpose(2, 1, 0, 3)).reshape(8192, 2048)
    P4 = np.ascontiguousarray(w_proj.T).reshape(64, 128, 16, 128)
    WpjR = np.ascontiguousarray(P4.transpose(2, 1, 0, 3)).reshape(2048, 8192)

    in_maps = []
    for i in range(8):
        b_att = i // 2
        hsel = i % 2
        b_ffn = i % 4
        half = i // 4
        in_maps.append({
            "x_tma": np.ascontiguousarray(x[b_att]),
            "x_tm": np.ascontiguousarray(x[b_ffn, half * 1024:(half + 1) * 1024, :]),
            "wqkvT": np.ascontiguousarray(WqkvT[:, 3072 * hsel: 3072 * (hsel + 1)]),
            "wpr": Wr,
            "bproj_cols": bpc,
            "wfc_sl": np.ascontiguousarray(WfcR[1024 * i: 1024 * (i + 1)]),
            "wpj_sl": np.ascontiguousarray(WpjR[256 * i: 256 * (i + 1)]),
            "ident": ident,
        })
    return in_maps


def kernel(L0_attn_query, L0_attn_key, L0_attn_value, L0_attn_Wqkv,
           L0_attn_Wproj, L0_attn_bproj, L0_ffn_w_fc, L0_ffn_w_proj):
    in_maps = _stage_inputs(L0_attn_query, L0_attn_Wqkv, L0_attn_Wproj,
                            L0_attn_bproj, L0_ffn_w_fc, L0_ffn_w_proj)
    nc = _build()
    res = run_bass_kernel_spmd(nc, in_maps, core_ids=list(range(8)))
    o = np.empty((NB, S, H), dtype=np.float32)
    for i in range(8):
        b_ffn = i % 4
        half = i // 4
        o[b_ffn, half * 1024:(half + 1) * 1024, :] = res.results[i]["out"]
    return o


